# version 07
# baseline (speedup 1.0000x reference)
import math
import contextlib
import numpy as np
import concourse.bass as bass
import concourse.mybir as mybir
from concourse.bass_utils import run_bass_kernel_spmd

F32 = mybir.dt.float32
BF16 = mybir.dt.bfloat16
AF = mybir.ActivationFunctionType
ALU = mybir.AluOpType

D = 2048
NH = 16
T = 8192
TQ = 2048
FF = 8192
NDT = 16
EPS = 1e-5
SCALE = 1.0 / math.sqrt(128.0)
EXPB = -4.0
DM = 16


class Sched:
    ENGINES = ("sync", "tensor", "scalar", "vector", "gpsimd")

    def __init__(self, nc):
        self.nc = nc
        self.ops = {e: [] for e in self.ENGINES}
        self.sem_handles = {}
        self.counts = {}
        self._stack = contextlib.ExitStack()

    def sem(self, name):
        if name not in self.sem_handles:
            self.sem_handles[name] = self._stack.enter_context(self.nc.semaphore(name))
            self.counts[name] = 0
        return self.sem_handles[name]

    def emit(self, engine, fn, waits=(), inc=None):
        w = []
        for s, v in waits:
            if v is None or v <= 0:
                continue
            self.sem(s)
            w.append((s, v))
        val = None
        if inc is not None:
            s, amt = inc
            self.sem(s)
            self.counts[s] += amt
            val = self.counts[s]
            inc = (s, amt)
        self.ops[engine].append((fn, w, inc))
        return val

    def val(self, name):
        return self.counts.get(name, 0)

    def build_block(self):
        nc = self.nc
        sems = self.sem_handles
        with nc.Block() as block:
            def mk_body(engine_name):
                def body(eng):
                    for fn, waits, inc in self.ops[engine_name]:
                        for s, v in waits:
                            eng.wait_ge(sems[s], v)
                        if fn is None:
                            continue
                        instr = fn(eng)
                        if inc is not None:
                            s, amt = inc
                            instr.then_inc(sems[s], amt)
                return body
            block.sync(mk_body("sync"))
            block.tensor(mk_body("tensor"))
            block.scalar(mk_body("scalar"))
            block.vector(mk_body("vector"))
            block.gpsimd(mk_body("gpsimd"))

    def close(self):
        self._stack.close()


def build_program(dbg=()):
    nc = bass.Bass("TRN2", target_bir_lowering=False)

    x_in = nc.dram_tensor("x", [T, D], F32, kind="ExternalInput").ap()
    wq = nc.dram_tensor("wq", [D, D], F32, kind="ExternalInput").ap()
    wk = nc.dram_tensor("wk", [D, D], F32, kind="ExternalInput").ap()
    wv = nc.dram_tensor("wv", [D, D], F32, kind="ExternalInput").ap()
    wo = nc.dram_tensor("wo", [D, D], F32, kind="ExternalInput").ap()
    wg = nc.dram_tensor("wg", [D, FF], F32, kind="ExternalInput").ap()
    wu = nc.dram_tensor("wu", [D, FF], F32, kind="ExternalInput").ap()
    wd = nc.dram_tensor("wd", [FF, D], F32, kind="ExternalInput").ap()
    out = nc.dram_tensor("out", [TQ, D], F32, kind="ExternalOutput").ap()

    dbg_outs = {}
    for name, shape, dt in [
        ("d_xn1T", [128, NDT * 2048], BF16),
        ("d_qT", [D, TQ], BF16),
        ("d_kb", [D, T], BF16),
        ("d_vb", [128, NH * T], BF16),
        ("d_attnT", [128, NDT * TQ], BF16),
        ("d_oA", [128, NDT * TQ], BF16),
        ("d_x1", [D, TQ], BF16),
        ("d_xn2T", [128, NDT * TQ], BF16),
        ("d_hsp", [FF, TQ], BF16),
        ("d_outT", [D, TQ], BF16),
    ]:
        if name in dbg:
            dbg_outs[name] = nc.dram_tensor(name, shape, dt, kind="ExternalOutput").ap()

    x_bf = nc.dram_tensor("x_bf", [T, D], BF16, kind="Internal").ap()
    wq_bf = nc.dram_tensor("wq_bf", [D, D], BF16, kind="Internal").ap()
    wk_bf = nc.dram_tensor("wk_bf", [D, D], BF16, kind="Internal").ap()
    wv_bf = nc.dram_tensor("wv_bf", [D, D], BF16, kind="Internal").ap()
    wo_bf = nc.dram_tensor("wo_bf", [D, D], BF16, kind="Internal").ap()
    qT_bf = nc.dram_tensor("qT_bf", [D, TQ], BF16, kind="Internal").ap()
    Kb = nc.dram_tensor("Kb", [D, T], BF16, kind="Internal").ap()
    Vb = nc.dram_tensor("Vb", [128, NH * T], BF16, kind="Internal").ap()
    x1_bf = nc.dram_tensor("x1_bf", [D, TQ], BF16, kind="Internal").ap()
    hsp_bf = nc.dram_tensor("hsp_bf", [FF, TQ], BF16, kind="Internal").ap()
    outT_bf = nc.dram_tensor("outT_bf", [D, TQ], BF16, kind="Internal").ap()

    S = Sched(nc)
    E = S.emit

    with contextlib.ExitStack() as st:
        def sb(name, shape, dt, stk=st):
            return stk.enter_context(nc.sbuf_tensor(name, shape, dt))

        cst = sb("cst", [128, 4], F32)
        ones_c = sb("ones_c", [128, 1], BF16)
        ones_r = sb("ones_r", [1, 128], BF16)
        srow = sb("srow", [1, 3 * D], F32)
        rowb = sb("rowb", [1, 2 * D], BF16)
        aB = sb("aB", [128, D], BF16)
        bB = sb("bB", [128, D], BF16)

        E("vector", lambda v: v.memset(cst[:, 0:1], EXPB))
        E("vector", lambda v: v.memset(cst[:, 1:2], EPS))
        E("vector", lambda v: v.memset(ones_c[:, :], 1.0))
        cst_v = E("vector", lambda v: v.memset(ones_r[:, :], 1.0), inc=("V", 1))

        def stage(dst, src, rows, semname, nchunks=4):
            step = rows // nchunks
            for i in range(nchunks):
                a, b = i * step, (i + 1) * step
                E("gpsimd", lambda g, a=a, b=b, dst=dst, src=src:
                  g.dma_start(dst[a:b, :], src[a:b, :]), inc=(semname, DM))
            return S.val(semname)

        x_stq = []
        for tbq in range(4):
            a, b = tbq * 2048, (tbq + 1) * 2048
            E("gpsimd", lambda g, a=a, b=b: g.dma_start(x_bf[a:b, :], x_in[a:b, :]),
              inc=(f"x_st{tbq}", DM))
            x_stq.append(S.val(f"x_st{tbq}"))
            if tbq == 0:
                wk_st = stage(wk_bf, wk, D, "wk_st", 2)
                wv_st = stage(wv_bf, wv, D, "wv_st", 2)
        wq_st = stage(wq_bf, wq, D, "wq_st", 2)
        wo_st = stage(wo_bf, wo, D, "wo_st", 2)
        wd_bf = nc.dram_tensor("wd_bf", [FF, D], BF16, kind="Internal").ap()
        wdst_v = None

        def ln_stats_math(n, inv_n):
            s_sum = srow[0:1, 0:n]
            s_sq = srow[0:1, D:D + n]
            t2 = srow[0:1, 2 * D:2 * D + n]
            E("vector", lambda v: v.tensor_scalar_mul(t2, s_sum, inv_n))
            E("vector", lambda v: v.tensor_mul(t2, t2, t2))
            vv = E("vector", lambda v: v.scalar_tensor_tensor(
                t2, s_sq, inv_n, t2, ALU.mult, ALU.subtract), inc=("V", 1))
            cc = E("scalar", lambda se: se.activation(s_sq, t2, AF.Ln, bias=cst[0:1, 1:2]),
                   waits=[("V", vv)], inc=("C", 1))
            cc = E("scalar", lambda se: se.activation(t2, s_sq, AF.Exp, scale=-0.5),
                   inc=("C", 1))
            E("vector", lambda v: v.scalar_tensor_tensor(
                s_sq, s_sum, -inv_n, t2, ALU.mult, ALU.mult), waits=[("C", cc)])
            E("vector", lambda v: v.tensor_copy(rowb[0:1, 0:n], t2))
            vv = E("vector", lambda v: v.tensor_copy(rowb[0:1, D:D + n], s_sq),
                   inc=("V", 1))
            return vv

        def ln_broadcast(n, ps_bc, v_rows, prev_apply_v):
            cc = None
            for c0 in range(0, n, 512):
                for which, dstT in ((0, aB), (1, bB)):
                    row = rowb[0:1, which * D + c0:which * D + c0 + 512]
                    pp = E("tensor", lambda t, row=row: t.matmul(
                        ps_bc[:, :], ones_r[:, :], row, start=True, stop=True),
                        waits=[("V", v_rows), ("C", S.val("C"))], inc=("P", 1))
                    cc = E("scalar", lambda se, dstT=dstT, c0=c0: se.copy(
                        dstT[:, c0:c0 + 512], ps_bc[:, :]),
                        waits=[("P", pp)] + ([("V", prev_apply_v)] if c0 == 0 and which == 0 else []),
                        inc=("C", 1))
            return cc

        v_apply_last = 0

        with contextlib.ExitStack() as stA:
            bigA = sb("bigA", [128, NDT * 2048], BF16, stA)

            with contextlib.ExitStack() as st0:
                sqT = [sb(f"sqT{i}", [128, 2048], BF16, st0) for i in range(2)]
                wTh = [sb(f"wTh{i}", [128, NDT * 1024], BF16, st0) for i in range(2)]
                ksb = [sb(f"ksb{i}", [128, 2048], BF16, st0) for i in range(2)]

                p_end_tb = 0
                v_apply_done = 0
                evac_c = [0, 0]
                store_g = [0, 0]
                p_sqmm = {}

                p_buf_free = [0, 0]

                def load_wTh(wbf, half, buf, stv, stn):
                    for dc in range(NDT):
                        E("sync", lambda s, dc=dc, wbf=wbf, half=half, buf=buf: s.dma_start_transpose(
                            wTh[buf][:, dc * 1024:(dc + 1) * 1024],
                            wbf[half * 1024:(half + 1) * 1024, dc * 128:(dc + 1) * 128]),
                          waits=([(stn, stv), ("P", p_buf_free[buf])]
                                 if dc == 0 else []),
                          inc=(f"wth{buf}", DM))
                    return S.val(f"wth{buf}")

                for tb in range(4):
                    xsem = f"xt{tb % 2}"
                    for dc in range(NDT):
                        E("sync", lambda s, tb=tb, dc=dc: s.dma_start_transpose(
                            bigA[:, dc * 2048:(dc + 1) * 2048],
                            x_bf[tb * 2048:(tb + 1) * 2048, dc * 128:(dc + 1) * 128]),
                          waits=([(f"x_st{tb}", x_stq[tb]), ("P", p_end_tb), ("V", v_apply_done),
                                  ("C", S.val("C")), ("st_o", S.val("st_o"))]
                                 if dc == 0 else []),
                          inc=(xsem, DM))
                    xt_v = S.val(xsem)

                    with (nc.psum_tensor(f"psSum{tb}", [1, 2048], F32) as psSum,
                          nc.psum_tensor(f"psSq{tb}", [1, 2048], F32) as psSq):
                        pp = None
                        for dc in range(NDT):
                            w = [(xsem, xt_v), ("V", cst_v)] if dc == 0 else []
                            if dc >= 2:
                                w = [("P", p_sqmm.get((tb, dc - 2), 0))]
                            vv = E("vector", lambda v, dc=dc: v.tensor_mul(
                                sqT[dc % 2][:, :], bigA[:, dc * 2048:(dc + 1) * 2048],
                                bigA[:, dc * 2048:(dc + 1) * 2048]),
                                waits=w, inc=("V", 1))
                            for c in range(4):
                                E("tensor", lambda t, dc=dc, c=c: t.matmul(
                                    psSum[0:1, c * 512:(c + 1) * 512], ones_c[:, :],
                                    bigA[:, dc * 2048 + c * 512:dc * 2048 + (c + 1) * 512],
                                    start=(dc == 0), stop=(dc == NDT - 1)),
                                    waits=([(xsem, xt_v), ("V", cst_v)] if (dc == 0 and c == 0) else []))
                                pp = E("tensor", lambda t, dc=dc, c=c: t.matmul(
                                    psSq[0:1, c * 512:(c + 1) * 512], ones_c[:, :],
                                    sqT[dc % 2][:, c * 512:(c + 1) * 512],
                                    start=(dc == 0), stop=(dc == NDT - 1)),
                                    waits=([("V", vv)] if c == 0 else []), inc=("P", 1))
                            p_sqmm[(tb, dc)] = pp
                        E("vector", lambda v: v.tensor_copy(srow[0:1, 0:2048], psSum[0:1, :]),
                          waits=[("P", pp)])
                        E("vector", lambda v: v.tensor_copy(srow[0:1, D:2 * D], psSq[0:1, :]),
                          inc=("V", 1))
                    v_rows = ln_stats_math(2048, 1.0 / D)
                    with nc.psum_tensor(f"psBc{tb}", [128, 512], F32) as psBc:
                        cc = ln_broadcast(2048, psBc, v_rows, v_apply_last)
                    for dc in range(NDT):
                        for c in range(4):
                            sl = slice(dc * 2048 + c * 512, dc * 2048 + (c + 1) * 512)
                            E("vector", lambda v, sl=sl, c=c: v.tensor_mul(
                                sqT[0][:, c * 512:(c + 1) * 512], bigA[:, sl],
                                aB[:, c * 512:(c + 1) * 512]),
                                waits=([("C", cc), ("P", S.val("P"))] if (dc == 0 and c == 0) else []))
                            v_apply_done = E("vector", lambda v, sl=sl, c=c: v.tensor_add(
                                bigA[:, sl], sqT[0][:, c * 512:(c + 1) * 512],
                                bB[:, c * 512:(c + 1) * 512]), inc=("V", 1))
                    v_apply_last = v_apply_done

                    def proj_eT(dst_dram, dst_cols, tag, wv_half, buf_of_half):
                        nonlocal evac_c, store_g
                        with (nc.psum_tensor(f"psP{tag}a", [128, 2048], F32) as psPa,
                              nc.psum_tensor(f"psP{tag}b", [128, 2048], F32) as psPb):
                            psP = [psPa, psPb]
                            p_half_done = [0, 0]
                            for eb in range(NDT):
                                half = eb // 8
                                buf = buf_of_half[half]
                                par = eb % 2
                                pp = None
                                for dc in range(NDT):
                                    for c in range(4):
                                        w = []
                                        if dc == 0 and c == 0:
                                            w = [(f"wth{buf}", wv_half[half]),
                                                 ("C", evac_c[par] if eb >= 2 else max(evac_c)),
                                                 ("V", v_apply_done)]
                                        pp = E("tensor", lambda t, dc=dc, c=c, eb=eb, par=par, buf=buf: t.matmul(
                                            psP[par][:, c * 512:(c + 1) * 512],
                                            wTh[buf][:, dc * 1024 + (eb % 8) * 128:dc * 1024 + (eb % 8) * 128 + 128],
                                            bigA[:, dc * 2048 + c * 512:dc * 2048 + (c + 1) * 512],
                                            start=(dc == 0), stop=(dc == NDT - 1)),
                                            waits=w,
                                            inc=(("P", 1) if (dc == NDT - 1 and c == 3) else None))
                                p_half_done[half] = pp
                                cc_l = None
                                for c in range(4):
                                    cc_l = E("scalar", lambda se, c=c, par=par: se.copy(
                                        ksb[par][:, c * 512:(c + 1) * 512],
                                        psP[par][:, c * 512:(c + 1) * 512]),
                                        waits=([("P", pp), ("st_sp", store_g[par])] if c == 0 else []),
                                        inc=("C", 1))
                                evac_c[par] = cc_l
                                store_g[par] = E("gpsimd", lambda g, eb=eb, par=par,
                                                 dst_dram=dst_dram, dst_cols=dst_cols: g.dma_start(
                                    dst_dram[eb * 128:(eb + 1) * 128, dst_cols], ksb[par][:, :]),
                                    waits=[("C", evac_c[par])], inc=("st_sp", DM))
                            return p_half_done

                    def proj_V(tb, wv_half, buf_of_half):
                        nonlocal evac_c, store_g
                        with (nc.psum_tensor(f"psV{tb}a", [128, 1024], F32) as psVa,
                              nc.psum_tensor(f"psV{tb}b", [128, 1024], F32) as psVb):
                            psV = [psVa, psVb]
                            p_half_done = [0, 0]
                            for cp in range(2):
                                buf = buf_of_half[cp]
                                for kb_i in range(16):
                                    par = kb_i % 2
                                    kt_glob = tb * 16 + kb_i
                                    pp = None
                                    for dc in range(NDT):
                                        for c in range(2):
                                            w = []
                                            if dc == 0 and c == 0:
                                                w = [(f"wth{buf}", wv_half[cp]),
                                                     ("C", evac_c[par] if (cp, kb_i) != (0, 0) else max(evac_c))]
                                            pp = E("tensor", lambda t, dc=dc, c=c, kb_i=kb_i, par=par, buf=buf: t.matmul(
                                                psV[par][:, c * 512:(c + 1) * 512],
                                                bigA[:, dc * 2048 + kb_i * 128:dc * 2048 + kb_i * 128 + 128],
                                                wTh[buf][:, dc * 1024 + c * 512:dc * 1024 + (c + 1) * 512],
                                                start=(dc == 0), stop=(dc == NDT - 1)),
                                                waits=w,
                                                inc=(("P", 1) if (dc == NDT - 1 and c == 1) else None))
                                    cc_l = None
                                    for c in range(2):
                                        cc_l = E("scalar", lambda se, c=c, par=par: se.copy(
                                            ksb[par][:, c * 512:(c + 1) * 512],
                                            psV[par][:, c * 512:(c + 1) * 512]),
                                            waits=([("P", pp), ("st_sp", store_g[par])] if c == 0 else []),
                                            inc=("C", 1))
                                    evac_c[par] = cc_l
                                    vdst = Vb[:, :].rearrange("p (h kte) -> p h kte", h=NH)[
                                        :, cp * 8:(cp + 1) * 8, kt_glob * 128:(kt_glob + 1) * 128]
                                    vsrc = ksb[par][:, 0:1024].rearrange("p (h e) -> p h e", e=128)
                                    store_g[par] = E("gpsimd", lambda g, vdst=vdst, vsrc=vsrc: g.dma_start(
                                        vdst, vsrc), waits=[("C", evac_c[par])], inc=("st_sp", DM))
                                p_half_done[cp] = S.val("P")
                            return p_half_done

                    wkh = [0, 0]
                    wkh[0] = load_wTh(wk_bf, 0, 0, wk_st, "wk_st")
                    wkh[1] = load_wTh(wk_bf, 1, 1, wk_st, "wk_st")
                    ph = proj_eT(Kb, slice(tb * 2048, (tb + 1) * 2048), f"k{tb}", wkh, [0, 1])
                    p_buf_free[0] = ph[0]
                    wvh = [0, 0]
                    wvh[0] = load_wTh(wv_bf, 0, 0, wv_st, "wv_st")
                    p_buf_free[1] = ph[1]
                    wvh[1] = load_wTh(wv_bf, 1, 1, wv_st, "wv_st")
                    phv = proj_V(tb, wvh, [0, 1])
                    p_buf_free[0] = phv[0]
                    p_buf_free[1] = phv[1]
                    if tb == 0:
                        wqh = [0, 0]
                        wqh[0] = load_wTh(wq_bf, 0, 0, wq_st, "wq_st")
                        wqh[1] = load_wTh(wq_bf, 1, 1, wq_st, "wq_st")
                        ph = proj_eT(qT_bf, slice(0, 2048), "q", wqh, [0, 1])
                        p_buf_free[0] = ph[0]
                        p_buf_free[1] = ph[1]
                        if "d_xn1T" in dbg_outs:
                            E("gpsimd", lambda g: g.dma_start(dbg_outs["d_xn1T"], bigA[:, :]),
                              waits=[("V", v_apply_done)], inc=("st_o", DM))
                    p_end_tb = S.val("P")

            sp_total = S.val("st_sp")
            wdst_v = stage(wd_bf, wd, FF, "wdst", 8)
            for dn, src_ap in (("d_qT", qT_bf), ("d_kb", Kb), ("d_vb", Vb)):
                if dn in dbg_outs:
                    E("gpsimd", lambda g, dn=dn, src_ap=src_ap: g.dma_start(dbg_outs[dn], src_ap),
                      waits=[("st_sp", sp_total)], inc=("st_o", DM))

            with contextlib.ExitStack() as stF:
                sK = [sb(f"sK{i}", [128, T], BF16, stF) for i in range(2)]
                sV = [sb(f"sV{i}", [128, T], BF16, stF) for i in range(2)]
                sQ = [sb(f"sQ{i}", [128, TQ], BF16, stF) for i in range(2)]
                sE = [sb(f"sE{i}", [128, 1024], BF16, stF) for i in range(4)]
                acc8 = sb("acc8", [128, 1024], BF16, stF)
                accF = sb("accF", [128, 1024], F32, stF)
                accB = sb("accB", [128, 1024], BF16, stF)
                bcF = sb("bcF", [128, 1024], F32, stF)
                recB = sb("recB", [1, 1024], BF16, stF)
                with (nc.psum_tensor("psS0", [128, 1024], F32) as psS0,
                      nc.psum_tensor("psS1", [128, 1024], F32) as psS1,
                      nc.psum_tensor("psO", [128, 1024], F32) as psO,
                      nc.psum_tensor("psX", [128, 1024], F32) as psX):
                    psS = [psS0, psS1]

                    p_head_end = [0, 0]
                    kv_v = [0, 0]

                    def load_head(h):
                        slot = h % 2
                        semn = f"kv{slot}"
                        E("sync", lambda s, h=h, slot=slot: s.dma_start(
                            sK[slot][:, :], Kb[h * 128:(h + 1) * 128, :]),
                          waits=[("st_sp", sp_total), ("P", p_head_end[slot])],
                          inc=(semn, DM))
                        E("sync", lambda s, h=h, slot=slot: s.dma_start(
                            sV[slot][:, :], Vb[:, h * T:(h + 1) * T]), inc=(semn, DM))
                        E("sync", lambda s, h=h, slot=slot: s.dma_start(
                            sQ[slot][:, :], qT_bf[h * 128:(h + 1) * 128, :]), inc=(semn, DM))
                        kv_v[slot] = S.val(semn)

                    load_head(0)
                    load_head(1)

                    c_ep_copy = 0
                    v_attn = 0
                    p_ep_pe = 0
                    prev_exp_last = 0
                    prev_pv_last = 0
                    prev_den_last = 0

                    for h in range(NH):
                        slot = h % 2
                        semn = f"kv{slot}"
                        kvw = kv_v[slot]
                        for qh in range(2):
                            q0 = qh * 1024
                            c_exp = {}
                            p_s = {}
                            p_pv = {}
                            v_den = {}
                            def emit_S(kt):
                                par = kt % 2
                                w = []
                                if kt == 0:
                                    w = [(semn, kvw), ("C", max(c_ep_copy, prev_exp_last)),
                                         ("V", max(v_attn, prev_den_last))]
                                elif kt >= 2:
                                    w = [("C", c_exp[kt - 2])]
                                E("tensor", lambda t, kt=kt, par=par, slot=slot, q0=q0: t.matmul(
                                    psS[par][:, 0:512], sK[slot][:, kt * 128:(kt + 1) * 128],
                                    sQ[slot][:, q0:q0 + 512], start=True, stop=True), waits=w)
                                p_s[kt] = E("tensor", lambda t, kt=kt, par=par, slot=slot, q0=q0: t.matmul(
                                    psS[par][:, 512:1024], sK[slot][:, kt * 128:(kt + 1) * 128],
                                    sQ[slot][:, q0 + 512:q0 + 1024], start=True, stop=True),
                                    inc=("P", 1))

                            emit_S(0)
                            emit_S(1)
                            for kt in range(64):
                                par = kt % 2
                                pe4 = kt % 4
                                w = [("P", p_s[kt])]
                                if kt < 4:
                                    w += [("P", prev_pv_last), ("V", prev_den_last)]
                                else:
                                    w += [("P", p_pv[kt - 4]), ("V", v_den[kt - 4])]
                                c_exp[kt] = E("scalar", lambda se, par=par, pe4=pe4: se.activation(
                                    sE[pe4][:, :], psS[par][:, :], AF.Exp,
                                    bias=cst[:, 0:1], scale=SCALE), waits=w, inc=("C", 1))
                                E("tensor", lambda t, kt=kt, pe4=pe4, slot=slot: t.matmul(
                                    psO[:, 0:512], sV[slot][:, kt * 128:(kt + 1) * 128],
                                    sE[pe4][:, 0:512], start=(kt == 0), stop=(kt == 63)),
                                    waits=[("C", c_exp[kt])])
                                p_pv[kt] = E("tensor", lambda t, kt=kt, pe4=pe4, slot=slot: t.matmul(
                                    psO[:, 512:1024], sV[slot][:, kt * 128:(kt + 1) * 128],
                                    sE[pe4][:, 512:1024], start=(kt == 0), stop=(kt == 63)),
                                    inc=("P", 1))
                                if kt + 2 < 64:
                                    emit_S(kt + 2)
                                w = [("C", c_exp[kt])]
                                ops = []
                                if kt % 8 == 0:
                                    ops.append(lambda v, pe4=pe4: v.tensor_copy(acc8[:, :], sE[pe4][:, :]))
                                else:
                                    ops.append(lambda v, pe4=pe4: v.tensor_add(
                                        acc8[:, :], acc8[:, :], sE[pe4][:, :]))
                                if kt % 8 == 7:
                                    if kt == 7:
                                        ops.append(lambda v: v.tensor_copy(accF[:, :], acc8[:, :]))
                                    else:
                                        ops.append(lambda v: v.tensor_add(accF[:, :], accF[:, :], acc8[:, :]))
                                for i, op in enumerate(ops):
                                    last = i == len(ops) - 1
                                    v_den[kt] = E("vector", op, waits=(w if i == 0 else []),
                                                  inc=(("V", 1) if last else None))
                            vv = E("vector", lambda v: v.tensor_copy(accB[:, :], accF[:, :]),
                                   waits=[("V", v_den[63]), ("P", p_ep_pe)], inc=("V", 1))
                            pp = None
                            for c in range(2):
                                pp = E("tensor", lambda t, c=c: t.matmul(
                                    psX[0:1, c * 512:(c + 1) * 512], ones_c[:, :],
                                    accB[:, c * 512:(c + 1) * 512], start=True, stop=True),
                                    waits=([("V", vv), ("C", c_ep_copy)] if c == 0 else []),
                                    inc=("P", 1))
                            vv = E("vector", lambda v: v.reciprocal(srow[0:1, 0:1024], psX[0:1, 0:1024]),
                                   waits=[("P", pp)], inc=("V", 1))
                            vv = E("vector", lambda v: v.tensor_copy(recB[0:1, :], srow[0:1, 0:1024]),
                                   inc=("V", 1))
                            pp = None
                            for c in range(2):
                                pp = E("tensor", lambda t, c=c: t.matmul(
                                    psX[:, c * 512:(c + 1) * 512], ones_r[:, :],
                                    recB[0:1, c * 512:(c + 1) * 512], start=True, stop=True),
                                    waits=([("V", vv)] if c == 0 else []), inc=("P", 1))
                            p_ep_pe = pp
                            c_ep_copy = E("scalar", lambda se: se.copy(bcF[:, :], psX[:, :]),
                                          waits=[("P", pp), ("V", v_attn)], inc=("C", 1))
                            v_attn = E("vector", lambda v, h=h, q0=q0: v.tensor_mul(
                                bigA[:, h * 2048 + q0:h * 2048 + q0 + 1024], psO[:, :], bcF[:, :]),
                                waits=[("C", c_ep_copy)], inc=("V", 1))
                            prev_exp_last = c_exp[63]
                            prev_pv_last = p_pv[63]
                            prev_den_last = v_den[63]
                        p_head_end[slot] = S.val("P")
                        if h + 2 < NH:
                            load_head(h + 2)

            flash_v = S.val("V")
            if "d_attnT" in dbg_outs:
                E("gpsimd", lambda g: g.dma_start(dbg_outs["d_attnT"], bigA[:, :]),
                  waits=[("V", flash_v)], inc=("st_o", DM))

            with contextlib.ExitStack() as stO:
                wTo = sb("wTo", [128, NDT * 2048], BF16, stO)
                tmp_o = sb("tmp_o", [128, 16 * 512], BF16, stO)
                sq5 = sb("sq5", [128, 512], BF16, stO)
                xrl = [sb(f"xrl{i}", [128, 512], BF16, stO) for i in range(2)]
                t5 = sb("t5o", [128, 512], BF16, stO)

                bP, bV = S.val("P"), S.val("V")
                for dc in range(NDT):
                    E("sync", lambda s, dc=dc: s.dma_start_transpose(
                        wTo[:, dc * 2048:(dc + 1) * 2048], wo_bf[:, dc * 128:(dc + 1) * 128]),
                      waits=([("wo_st", wo_st), ("P", bP), ("V", bV)] if dc == 0 else []),
                      inc=("wTo_s", DM))
                wto_v = S.val("wTo_s")

                v_wb_prev = 0
                p_sq_prev = 0
                with (nc.psum_tensor("psPo0", [128, 512], F32) as psPo0,
                      nc.psum_tensor("psPo1", [128, 512], F32) as psPo1,
                      nc.psum_tensor("psL2s", [1, 512], F32) as psL2s,
                      nc.psum_tensor("psL2q", [1, 512], F32) as psL2q):
                    psPo = [psPo0, psPo1]
                    evc = [0, 0]
                    v_stat_ev = 0
                    for tc in range(4):
                        for eg in range(NDT):
                            par = eg % 2
                            pp = None
                            for dc in range(NDT):
                                pp = E("tensor", lambda t, dc=dc, eg=eg, tc=tc, par=par: t.matmul(
                                    psPo[par][:, :],
                                    wTo[:, dc * 2048 + eg * 128:dc * 2048 + eg * 128 + 128],
                                    bigA[:, dc * 2048 + tc * 512:dc * 2048 + (tc + 1) * 512],
                                    start=(dc == 0), stop=(dc == NDT - 1)),
                                    waits=([("wTo_s", wto_v), ("C", evc[par]), ("V", flash_v)]
                                           if dc == 0 else []),
                                    inc=(("P", 1) if dc == NDT - 1 else None))
                            cc1 = E("scalar", lambda se, eg=eg, par=par: se.copy(
                                tmp_o[:, eg * 512:(eg + 1) * 512], psPo[par][:, :]),
                                waits=[("P", pp)] + ([("V", v_wb_prev)] if eg == 0 else []),
                                inc=("C", 1))
                            evc[par] = cc1
                            vv = E("vector", lambda v, eg=eg: v.tensor_mul(
                                sq5[:, :], tmp_o[:, eg * 512:(eg + 1) * 512],
                                tmp_o[:, eg * 512:(eg + 1) * 512]),
                                waits=[("C", cc1), ("P", p_sq_prev)], inc=("V", 1))
                            E("tensor", lambda t, eg=eg: t.matmul(
                                psL2s[0:1, 0:512], ones_c[:, :],
                                tmp_o[:, eg * 512:(eg + 1) * 512],
                                start=(eg == 0), stop=(eg == 15)),
                                waits=([("C", cc1)] + ([("V", v_stat_ev)] if eg == 0 else [])))
                            p_sq_prev = E("tensor", lambda t, eg=eg: t.matmul(
                                psL2q[0:1, 0:512], ones_c[:, :], sq5[:, :],
                                start=(eg == 0), stop=(eg == 15)),
                                waits=[("V", vv)], inc=("P", 1))
                        E("vector", lambda v, tc=tc: v.tensor_copy(
                            srow[0:1, tc * 512:(tc + 1) * 512], psL2s[0:1, :]),
                            waits=[("P", p_sq_prev)])
                        v_stat_ev = E("vector", lambda v, tc=tc: v.tensor_copy(
                            srow[0:1, D + tc * 512:D + (tc + 1) * 512], psL2q[0:1, :]),
                            inc=("V", 1))
                        for dt_ in range(NDT):
                            v_wb_prev = E("vector", lambda v, dt_=dt_, tc=tc: v.tensor_copy(
                                bigA[:, dt_ * 2048 + tc * 512:dt_ * 2048 + (tc + 1) * 512],
                                tmp_o[:, dt_ * 512:(dt_ + 1) * 512]),
                                waits=([("P", S.val("P")), ("st_o", S.val("st_o"))] if dt_ == 0 else []),
                                inc=("V", 1))
                    v_oA_wb = v_wb_prev
                    v_rows2 = ln_stats_math(2048, 1.0 / D)
                if "d_oA" in dbg_outs:
                    E("gpsimd", lambda g: g.dma_start(dbg_outs["d_oA"], bigA[:, :]),
                      waits=[("V", v_oA_wb)], inc=("st_o", DM))

                with nc.psum_tensor("psBcO2", [128, 512], F32) as psBc2:
                    cc2 = ln_broadcast(2048, psBc2, v_rows2, v_apply_last)

                v_xrl_used = [0, 0]
                x1st_v = 0
                with (nc.psum_tensor("psL3s", [1, 512], F32) as psL3s,
                      nc.psum_tensor("psL3q", [1, 512], F32) as psL3q):
                    v_stat_ev3 = 0
                    for tc in range(4):
                        for dt_ in range(NDT):
                            slot = (tc * NDT + dt_) % 2
                            xv = E("sync", lambda s, dt_=dt_, tc=tc, slot=slot: s.dma_start_transpose(
                                xrl[slot][:, :],
                                x_bf[tc * 512:(tc + 1) * 512, dt_ * 128:(dt_ + 1) * 128]),
                                waits=[("V", v_xrl_used[slot])], inc=(f"xrl{slot}", DM))
                            sl = slice(dt_ * 2048 + tc * 512, dt_ * 2048 + (tc + 1) * 512)
                            E("vector", lambda v, sl=sl, tc=tc: v.tensor_mul(
                                t5[:, :], bigA[:, sl], aB[:, tc * 512:(tc + 1) * 512]),
                                waits=([("C", cc2), ("st_o", S.val("st_o"))] if (tc == 0 and dt_ == 0) else []))
                            E("vector", lambda v, tc=tc: v.tensor_add(
                                t5[:, :], t5[:, :], bB[:, tc * 512:(tc + 1) * 512]))
                            vv = E("vector", lambda v, sl=sl, slot=slot: v.tensor_add(
                                bigA[:, sl], t5[:, :], xrl[slot][:, :]),
                                waits=[(f"xrl{slot}", xv)], inc=("V", 1))
                            v_xrl_used[slot] = vv
                            x1st_v = E("gpsimd", lambda g, dt_=dt_, tc=tc, sl=sl: g.dma_start(
                                x1_bf[dt_ * 128:(dt_ + 1) * 128, tc * 512:(tc + 1) * 512],
                                bigA[:, sl]), waits=[("V", vv)], inc=("x1_st", DM))
                            vv2 = E("vector", lambda v, sl=sl: v.tensor_mul(
                                sq5[:, :], bigA[:, sl], bigA[:, sl]),
                                waits=[("P", p_sq_prev)], inc=("V", 1))
                            E("tensor", lambda t, sl=sl, dt_=dt_: t.matmul(
                                psL3s[0:1, 0:512], ones_c[:, :], bigA[:, sl],
                                start=(dt_ == 0), stop=(dt_ == NDT - 1)),
                                waits=([("V", vv)] + ([("V", v_stat_ev3)] if dt_ == 0 else [])))
                            p_sq_prev = E("tensor", lambda t, dt_=dt_: t.matmul(
                                psL3q[0:1, 0:512], ones_c[:, :], sq5[:, :],
                                start=(dt_ == 0), stop=(dt_ == NDT - 1)),
                                waits=[("V", vv2)], inc=("P", 1))
                        E("vector", lambda v, tc=tc: v.tensor_copy(
                            srow[0:1, tc * 512:(tc + 1) * 512], psL3s[0:1, :]),
                            waits=[("P", p_sq_prev)])
                        v_stat_ev3 = E("vector", lambda v, tc=tc: v.tensor_copy(
                            srow[0:1, D + tc * 512:D + (tc + 1) * 512], psL3q[0:1, :]),
                            inc=("V", 1))
                    v_rows3 = ln_stats_math(2048, 1.0 / D)
                if "d_x1" in dbg_outs:
                    E("gpsimd", lambda g: g.dma_start(dbg_outs["d_x1"], x1_bf),
                      waits=[("x1_st", x1st_v)], inc=("st_o", DM))
                with nc.psum_tensor("psBcO3", [128, 512], F32) as psBc3:
                    cc3 = ln_broadcast(2048, psBc3, v_rows3, v_apply_last)
                v_xn2 = 0
                for dt_ in range(NDT):
                    for c in range(4):
                        sl = slice(dt_ * 2048 + c * 512, dt_ * 2048 + (c + 1) * 512)
                        E("vector", lambda v, sl=sl, c=c: v.tensor_mul(
                            t5[:, :], bigA[:, sl], aB[:, c * 512:(c + 1) * 512]),
                            waits=([("C", cc3), ("x1_st", x1st_v)] if (dt_ == 0 and c == 0) else []))
                        v_xn2 = E("vector", lambda v, sl=sl, c=c: v.tensor_add(
                            bigA[:, sl], t5[:, :], bB[:, c * 512:(c + 1) * 512]), inc=("V", 1))
                v_apply_last = v_xn2
            if "d_xn2T" in dbg_outs:
                E("gpsimd", lambda g: g.dma_start(dbg_outs["d_xn2T"], bigA[:, :]),
                  waits=[("V", v_xn2)], inc=("st_o", DM))

            with contextlib.ExitStack() as st1:
                wgs = [sb(f"wgs{i}", [128, NDT * 512], BF16, st1) for i in range(2)]
                wus = [sb(f"wus{i}", [128, NDT * 512], BF16, st1) for i in range(2)]
                hq = sb("hq", [128, 4 * 512], F32, st1)
                ha = sb("ha", [128, 4 * 512], F32, st1)
                hm = sb("hm", [128, 3 * 512], F32, st1)
                hsp4 = sb("hsp4", [128, 4 * 512], BF16, st1)
                sg = sb("sg", [128, 512], F32, st1)

                bP1, bV1, bC1 = S.val("P"), S.val("V"), S.val("C")
                p_ib_end = [bP1, bP1]

                def load_wgu(ib):
                    slot = ib % 2
                    semn = f"wgu{slot}"
                    src_g = wg[:, ib * 512:(ib + 1) * 512].rearrange("(hc p) i -> p hc i", p=128)
                    dst_g = wgs[slot][:, :].rearrange("p (hc i) -> p hc i", i=512)
                    E("gpsimd", lambda g, src_g=src_g, dst_g=dst_g: g.dma_start(dst_g, src_g),
                      waits=[("P", p_ib_end[slot])] + ([("V", bV1), ("C", bC1)] if ib < 2 else []),
                      inc=(semn, DM))
                    src_u = wu[:, ib * 512:(ib + 1) * 512].rearrange("(hc p) i -> p hc i", p=128)
                    dst_u = wus[slot][:, :].rearrange("p (hc i) -> p hc i", i=512)
                    E("gpsimd", lambda g, src_u=src_u, dst_u=dst_u: g.dma_start(dst_u, src_u),
                      inc=(semn, DM))
                    return S.val(semn)

                wguv = [load_wgu(0), load_wgu(1)]
                with (nc.psum_tensor("psG", [128, 2048], F32) as psG,
                      nc.psum_tensor("psU", [128, 2048], F32) as psU):
                    c_half = [0, 0]
                    v_half = [0, 0]
                    v_spars_last = 0
                    hsp_store_v = 0
                    for ib in range(16):
                        slot = ib % 2
                        for tc in range(4):
                            for jh in range(2):
                                pp = None
                                for j in (jh * 2, jh * 2 + 1):
                                    for dcw, ps in ((wgs[slot], psG), (wus[slot], psU)):
                                        for hc in range(NDT):
                                            lhsT = dcw[:, hc * 512 + j:hc * 512 + j + 509:4]
                                            w = []
                                            if hc == 0:
                                                w = [("C", c_half[jh]), ("V", v_half[jh])]
                                                if j == jh * 2 and ps is psG:
                                                    w.append((f"wgu{slot}", wguv[slot]))
                                                    if ib == 0 and tc == 0:
                                                        w.append(("V", v_xn2))
                                            pp = E("tensor", lambda t, j=j, tc=tc, ps=ps, lhsT=lhsT, hc=hc: t.matmul(
                                                ps[:, j * 512:(j + 1) * 512], lhsT,
                                                bigA[:, hc * 2048 + tc * 512:hc * 2048 + (tc + 1) * 512],
                                                start=(hc == 0), stop=(hc == NDT - 1)), waits=w,
                                                inc=(("P", 1) if hc == NDT - 1 else None))
                                vj = v_half[jh]
                                for j in (jh * 2, jh * 2 + 1):
                                    sg_guard = v_spars_last if jh == 0 else v_half[0]
                                    cc = E("scalar", lambda se, j=j: se.activation(
                                        sg[:, :], psG[:, j * 512:(j + 1) * 512], AF.Silu),
                                        waits=([("P", pp), ("V", sg_guard)] if j == jh * 2
                                               else [("V", vj)]),
                                        inc=("C", 1))
                                    vj = E("vector", lambda v, j=j: v.tensor_mul(
                                        hq[:, j * 512:(j + 1) * 512], sg[:, :],
                                        psU[:, j * 512:(j + 1) * 512]),
                                        waits=[("C", cc)], inc=("V", 1))
                                    cc2 = E("scalar", lambda se, j=j: se.activation(
                                        ha[:, j * 512:(j + 1) * 512],
                                        hq[:, j * 512:(j + 1) * 512], AF.Abs),
                                        waits=[("V", vj)], inc=("C", 1))
                                c_half[jh] = cc2
                                v_half[jh] = vj
                            a0, a1 = ha[:, 0:512], ha[:, 512:1024]
                            a2, a3 = ha[:, 1024:1536], ha[:, 1536:2048]
                            h0, h1, h2 = hm[:, 0:512], hm[:, 512:1024], hm[:, 1024:1536]
                            E("vector", lambda v: v.tensor_max(h0, a0, a1),
                              waits=[("C", c_half[1]), ("hsp_st", hsp_store_v)])
                            E("vector", lambda v: v.tensor_tensor(h1, a0, a1, ALU.min))
                            E("vector", lambda v: v.tensor_max(h2, a2, a3))
                            E("vector", lambda v: v.tensor_tensor(h0, h0, h2, ALU.min))
                            E("vector", lambda v: v.tensor_tensor(h2, a2, a3, ALU.min))
                            E("vector", lambda v: v.tensor_max(h1, h1, h2))
                            E("vector", lambda v: v.tensor_max(h0, h0, h1))
                            vlast = None
                            for j in range(4):
                                E("vector", lambda v, j=j: v.tensor_tensor(
                                    h1[:, :], ha[:, j * 512:(j + 1) * 512], h0, ALU.is_ge))
                                vlast = E("vector", lambda v, j=j: v.tensor_mul(
                                    hsp4[:, j * 512:(j + 1) * 512],
                                    hq[:, j * 512:(j + 1) * 512], h1[:, :]), inc=("V", 1))
                            v_spars_last = vlast
                            for j in range(4):
                                hsp_store_v = E("gpsimd", lambda g, ib=ib, j=j, tc=tc: g.dma_start(
                                    hsp_bf[(ib * 4 + j) * 128:(ib * 4 + j + 1) * 128,
                                           tc * 512:(tc + 1) * 512],
                                    hsp4[:, j * 512:(j + 1) * 512]),
                                    waits=([("V", vlast)] if j == 0 else []), inc=("hsp_st", DM))
                        p_ib_end[slot] = S.val("P")
                        if ib + 2 < 16:
                            wguv[slot] = load_wgu(ib + 2)
            hsp_total = S.val("hsp_st")
            if "d_hsp" in dbg_outs:
                E("gpsimd", lambda g: g.dma_start(dbg_outs["d_hsp"], hsp_bf),
                  waits=[("hsp_st", hsp_total)], inc=("st_o", DM))

        with contextlib.ExitStack() as st2:
            hspS = [sb(f"hspS{i}", [128, 16 * 1024], BF16, st2) for i in range(2)]
            wds = [sb(f"wds{i}", [128, 8 * 512], BF16, st2) for i in range(2)]
            yb = [sb(f"yb{i}", [128, NDT * 1024], BF16, st2) for i in range(2)]
            sqy = sb("sqy", [128, 1024], BF16, st2)
            x1r = [sb(f"x1r{i}", [128, 1024], BF16, st2) for i in range(2)]
            t6 = sb("t6", [128, 1024], BF16, st2)
            oT = [sb(f"oT{i}", [128, 1024], BF16, st2) for i in range(2)]
            finB = sb("finB", [128, 2 * 2048], BF16, st2)

            bP2, bV2, bC2 = S.val("P"), S.val("V"), S.val("C")
            out_total = 0
            v_app4 = 0
            fin_used = [0, 0]
            v_x1r_used = [bV2, bV2]
            vo_used = [0, 0]
            p_hsp_used = [bP2, bP2]
            hsp_vv = [0, 0]
            p_wd = [bP2, bP2]
            wd_v = [0, 0]

            def load_hsp_slab(tc, q):
                slot = q % 2
                semn = f"hspl{slot}"
                hsrc = hsp_bf[q * 16 * 128:(q + 1) * 16 * 128,
                              tc * 1024:(tc + 1) * 1024].rearrange(
                    "(iblk p) t -> p iblk t", p=128)
                hdst = hspS[slot][:, :].rearrange("p (iblk t) -> p iblk t", t=1024)
                E("sync", lambda s, hsrc=hsrc, hdst=hdst: s.dma_start(hdst, hsrc),
                  waits=[("hsp_st", hsp_total), ("P", p_hsp_used[slot])],
                  inc=(semn, DM))
                hsp_vv[slot] = S.val(semn)

            def load_wd(grp, eg):
                slot = grp % 2
                semn = f"wd{slot}"
                for ibl in range(2):
                    srcap = wd_bf[grp * 1024 + ibl * 512:grp * 1024 + (ibl + 1) * 512,
                                  eg * 512:(eg + 1) * 512].rearrange(
                        "(p four) e -> p four e", four=4)
                    dstap = wds[slot][:, ibl * 2048:(ibl + 1) * 2048].rearrange(
                        "p (four e) -> p four e", e=512)
                    E("gpsimd", lambda g, srcap=srcap, dstap=dstap: g.dma_start(dstap, srcap),
                      waits=([("P", p_wd[slot])] + ([("wdst", wdst_v)] if (grp < 2 and eg == 0) else [])
                             if ibl == 0 else []),
                      inc=(semn, DM))
                return S.val(semn)

            for tc in range(2):
                with contextlib.ExitStack() as stY:
                    psY = [stY.enter_context(nc.psum_tensor(f"psY{tc}_{i}", [128, 1024], F32))
                           for i in range(4)]
                    c_evY = S.val("C") if tc > 0 else 0
                    for eg in range(4):
                        load_hsp_slab(tc, 0)
                        load_hsp_slab(tc, 1)
                        wd_v[0] = load_wd(0, eg)
                        wd_v[1] = load_wd(1, eg)
                        pp = None
                        for q in range(4):
                            hslot = q % 2
                            for sub in range(2):
                                grp = q * 2 + sub
                                wslot = grp % 2
                                for ibl2 in range(2):
                                    for j in range(4):
                                        iblk = grp * 8 + ibl2 * 4 + j
                                        ibl_loc = (iblk - q * 16)
                                        for eb in range(4):
                                            for th in range(2):
                                                w = []
                                                if ibl2 == 0 and j == 0 and eb == 0 and th == 0:
                                                    w = [(f"wd{wslot}", wd_v[wslot]),
                                                         (f"hspl{hslot}", hsp_vv[hslot])]
                                                    if grp == 0:
                                                        w.append(("C", c_evY))
                                                        w.append(("V", v_app4))
                                                pp = E("tensor", lambda t, iblk=iblk, ibl2=ibl2, j=j,
                                                       eb=eb, th=th, wslot=wslot, hslot=hslot,
                                                       ibl_loc=ibl_loc: t.matmul(
                                                    psY[eb][:, th * 512:(th + 1) * 512],
                                                    wds[wslot][:, ibl2 * 2048 + j * 512 + eb * 128:
                                                               ibl2 * 2048 + j * 512 + eb * 128 + 128],
                                                    hspS[hslot][:, ibl_loc * 1024 + th * 512:
                                                                ibl_loc * 1024 + (th + 1) * 512],
                                                    start=(iblk == 0), stop=(iblk == 63)),
                                                    waits=w,
                                                    inc=(("P", 1) if (ibl2 == 1 and j == 3 and eb == 3 and th == 1) else None))
                                p_wd[wslot] = S.val("P")
                                if grp + 2 < 8:
                                    wd_v[wslot] = load_wd(grp + 2, eg)
                            p_hsp_used[hslot] = S.val("P")
                            if q + 2 < 4:
                                load_hsp_slab(tc, q + 2)
                        for eb in range(4):
                            c_evY = E("scalar", lambda se, eb=eb, eg=eg: se.copy(
                                yb[tc % 2][:, (eg * 4 + eb) * 1024:(eg * 4 + eb + 1) * 1024],
                                psY[eb][:, :]),
                                waits=([("P", pp)] if eb == 0 else []), inc=("C", 1))

                with (nc.psum_tensor(f"psL4s{tc}", [1, 1024], F32) as psL4s,
                      nc.psum_tensor(f"psL4q{tc}", [1, 1024], F32) as psL4q,
                      nc.psum_tensor(f"psBc4{tc}", [128, 512], F32) as psBc4):
                    p_sqy = 0
                    for et in range(NDT):
                        vv = E("vector", lambda v, et=et: v.tensor_mul(
                            sqy[:, :], yb[tc % 2][:, et * 1024:(et + 1) * 1024],
                            yb[tc % 2][:, et * 1024:(et + 1) * 1024]),
                            waits=[("C", c_evY), ("P", p_sqy)], inc=("V", 1))
                        for c in range(2):
                            E("tensor", lambda t, et=et, c=c: t.matmul(
                                psL4s[0:1, c * 512:(c + 1) * 512], ones_c[:, :],
                                yb[tc % 2][:, et * 1024 + c * 512:et * 1024 + (c + 1) * 512],
                                start=(et == 0), stop=(et == NDT - 1)),
                                waits=([("C", c_evY)] if (et == 0 and c == 0) else []))
                            p_sqy = E("tensor", lambda t, et=et, c=c: t.matmul(
                                psL4q[0:1, c * 512:(c + 1) * 512], ones_c[:, :],
                                sqy[:, c * 512:(c + 1) * 512],
                                start=(et == 0), stop=(et == NDT - 1)),
                                waits=([("V", vv)] if c == 0 else []),
                                inc=(("P", 1) if (et == NDT - 1 and c == 1) else None))
                    E("vector", lambda v: v.tensor_copy(srow[0:1, 0:1024], psL4s[0:1, :]),
                      waits=[("P", S.val("P"))])
                    E("vector", lambda v: v.tensor_copy(srow[0:1, D:D + 1024], psL4q[0:1, :]),
                      inc=("V", 1))
                    v_rows4 = ln_stats_math(1024, 1.0 / D)
                    cc4 = ln_broadcast(1024, psBc4, v_rows4, v_apply_last)
                    for et in range(NDT):
                        slot = et % 2
                        xv = E("sync", lambda s, et=et, tc=tc, slot=slot: s.dma_start(
                            x1r[slot][:, :],
                            x1_bf[et * 128:(et + 1) * 128, tc * 1024:(tc + 1) * 1024]),
                            waits=[("V", v_x1r_used[slot])] + ([("x1_st", S.val("x1_st"))] if et < 2 and tc == 0 else []),
                            inc=(f"x1r{slot}", DM))
                        for c in range(2):
                            E("vector", lambda v, et=et, c=c: v.tensor_mul(
                                t6[:, c * 512:(c + 1) * 512],
                                yb[tc % 2][:, et * 1024 + c * 512:et * 1024 + (c + 1) * 512],
                                aB[:, c * 512:(c + 1) * 512]),
                                waits=([("C", cc4)] if (et == 0 and c == 0) else []))
                            E("vector", lambda v, c=c: v.tensor_add(
                                t6[:, c * 512:(c + 1) * 512], t6[:, c * 512:(c + 1) * 512],
                                bB[:, c * 512:(c + 1) * 512]))
                        vv = E("vector", lambda v, slot=slot: v.tensor_add(
                            oT[slot][:, :], t6[:, :], x1r[slot][:, :]),
                            waits=[(f"x1r{slot}", xv), ("outT_st", vo_used[slot])], inc=("V", 1))
                        v_x1r_used[slot] = vv
                        v_app4 = vv
                        out_total = E("gpsimd", lambda g, et=et, tc=tc, slot=slot: g.dma_start(
                            outT_bf[et * 128:(et + 1) * 128, tc * 1024:(tc + 1) * 1024],
                            oT[slot][:, :]), waits=[("V", vv)], inc=("outT_st", DM))
                        vo_used[slot] = out_total

                for tt in range(tc * 8, (tc + 1) * 8):
                    slot = tt % 2
                    tv = E("sync", lambda s, tt=tt, slot=slot: s.dma_start_transpose(
                        finB[:, slot * 2048:(slot + 1) * 2048],
                        outT_bf[:, tt * 128:(tt + 1) * 128]),
                        waits=[("outT_st", out_total), ("out_st", fin_used[slot])],
                        inc=(f"fin{slot}", DM))
                    fin_used[slot] = E("gpsimd", lambda g, tt=tt, slot=slot: g.dma_start(
                        out[tt * 128:(tt + 1) * 128, :], finB[:, slot * 2048:(slot + 1) * 2048]),
                        waits=[(f"fin{slot}", tv)], inc=("out_st", DM))

            if "d_outT" in dbg_outs:
                E("gpsimd", lambda g: g.dma_start(dbg_outs["d_outT"], outT_bf),
                  waits=[("outT_st", out_total)], inc=("st_o", DM))
            E("sync", None, waits=[("out_st", S.val("out_st")), ("st_o", S.val("st_o"))])
            E("gpsimd", None, waits=[("out_st", S.val("out_st"))])

        S.build_block()
        S.close()
    return nc


_NC_CACHE = {}


def kernel(L0_attn_x, L0_attn_w_q, L0_attn_w_k, L0_attn_w_v, L0_attn_w_o,
           L0_ffn_w_gate, L0_ffn_w_up, L0_ffn_w_down):
    x = np.ascontiguousarray(np.asarray(L0_attn_x, dtype=np.float32))
    ws = {k: np.ascontiguousarray(np.asarray(v, dtype=np.float32)) for k, v in dict(
        wq=L0_attn_w_q, wk=L0_attn_w_k, wv=L0_attn_w_v, wo=L0_attn_w_o,
        wg=L0_ffn_w_gate, wu=L0_ffn_w_up, wd=L0_ffn_w_down).items()}
    if "main" not in _NC_CACHE:
        _NC_CACHE["main"] = build_program()
    nc = _NC_CACHE["main"]
    in_maps = []
    for c in range(8):
        b, qi = c // 4, c % 4
        xb = np.roll(x[b], -qi * TQ, axis=0)
        m = dict(x=np.ascontiguousarray(xb))
        m.update(ws)
        in_maps.append(m)
    res = run_bass_kernel_spmd(nc, in_maps, core_ids=list(range(8)))
    full = np.empty((2, T, D), dtype=np.float32)
    for c in range(8):
        b, qi = c // 4, c % 4
        full[b, qi * TQ:(qi + 1) * TQ, :] = res.results[c]["out"]
    return full


# version 08
# speedup vs baseline: 1.1564x; 1.1564x over previous
import math
import contextlib
import numpy as np
import concourse.bass as bass
import concourse.mybir as mybir
from concourse.bass_utils import run_bass_kernel_spmd

F32 = mybir.dt.float32
BF16 = mybir.dt.bfloat16
AF = mybir.ActivationFunctionType
ALU = mybir.AluOpType

D = 2048
NH = 16
T = 8192
TQ = 2048
FF = 8192
NDT = 16
EPS = 1e-5
SCALE = 1.0 / math.sqrt(128.0)
EXPB = -4.0
DM = 16


class Sched:
    ENGINES = ("sync", "tensor", "scalar", "vector", "gpsimd")

    def __init__(self, nc):
        self.nc = nc
        self.ops = {e: [] for e in self.ENGINES}
        self.sem_handles = {}
        self.counts = {}
        self._stack = contextlib.ExitStack()

    def sem(self, name):
        if name not in self.sem_handles:
            self.sem_handles[name] = self._stack.enter_context(self.nc.semaphore(name))
            self.counts[name] = 0
        return self.sem_handles[name]

    def emit(self, engine, fn, waits=(), inc=None):
        w = []
        for s, v in waits:
            if v is None or v <= 0:
                continue
            self.sem(s)
            w.append((s, v))
        val = None
        if inc is not None:
            s, amt = inc
            self.sem(s)
            self.counts[s] += amt
            val = self.counts[s]
            inc = (s, amt)
        self.ops[engine].append((fn, w, inc))
        return val

    def val(self, name):
        return self.counts.get(name, 0)

    def build_block(self):
        nc = self.nc
        sems = self.sem_handles
        with nc.Block() as block:
            def mk_body(engine_name):
                def body(eng):
                    for fn, waits, inc in self.ops[engine_name]:
                        for s, v in waits:
                            eng.wait_ge(sems[s], v)
                        if fn is None:
                            continue
                        instr = fn(eng)
                        if inc is not None:
                            s, amt = inc
                            instr.then_inc(sems[s], amt)
                return body
            block.sync(mk_body("sync"))
            block.tensor(mk_body("tensor"))
            block.scalar(mk_body("scalar"))
            block.vector(mk_body("vector"))
            block.gpsimd(mk_body("gpsimd"))

    def close(self):
        self._stack.close()


def build_program(dbg=()):
    nc = bass.Bass("TRN2", target_bir_lowering=False)

    x_in = nc.dram_tensor("x", [T, D], F32, kind="ExternalInput").ap()
    wq = nc.dram_tensor("wq", [D, D], F32, kind="ExternalInput").ap()
    wk = nc.dram_tensor("wk", [D, D], F32, kind="ExternalInput").ap()
    wv = nc.dram_tensor("wv", [D, D], F32, kind="ExternalInput").ap()
    wo = nc.dram_tensor("wo", [D, D], F32, kind="ExternalInput").ap()
    wg = nc.dram_tensor("wg", [D, FF], F32, kind="ExternalInput").ap()
    wu = nc.dram_tensor("wu", [D, FF], F32, kind="ExternalInput").ap()
    wd = nc.dram_tensor("wd", [FF, D], F32, kind="ExternalInput").ap()
    out = nc.dram_tensor("out", [TQ, D], F32, kind="ExternalOutput").ap()

    dbg_outs = {}
    for name, shape, dt in [
        ("d_xn1T", [128, NDT * 2048], BF16),
        ("d_qT", [D, TQ], BF16),
        ("d_kb", [D, T], BF16),
        ("d_vb", [128, NH * T], BF16),
        ("d_attnT", [128, NDT * TQ], BF16),
        ("d_oA", [128, NDT * TQ], BF16),
        ("d_x1", [D, TQ], BF16),
        ("d_xn2T", [128, NDT * TQ], BF16),
        ("d_hsp", [FF, TQ], BF16),
        ("d_outT", [D, TQ], BF16),
    ]:
        if name in dbg:
            dbg_outs[name] = nc.dram_tensor(name, shape, dt, kind="ExternalOutput").ap()

    x_bf = nc.dram_tensor("x_bf", [T, D], BF16, kind="Internal").ap()
    wq_bf = nc.dram_tensor("wq_bf", [D, D], BF16, kind="Internal").ap()
    wk_bf = nc.dram_tensor("wk_bf", [D, D], BF16, kind="Internal").ap()
    wv_bf = nc.dram_tensor("wv_bf", [D, D], BF16, kind="Internal").ap()
    wo_bf = nc.dram_tensor("wo_bf", [D, D], BF16, kind="Internal").ap()
    qT_bf = nc.dram_tensor("qT_bf", [D, TQ], BF16, kind="Internal").ap()
    Kb = nc.dram_tensor("Kb", [D, T], BF16, kind="Internal").ap()
    Vb = nc.dram_tensor("Vb", [128, NH * T], BF16, kind="Internal").ap()
    x1_bf = nc.dram_tensor("x1_bf", [D, TQ], BF16, kind="Internal").ap()
    hsp_bf = nc.dram_tensor("hsp_bf", [FF, TQ], BF16, kind="Internal").ap()
    outT_bf = nc.dram_tensor("outT_bf", [D, TQ], BF16, kind="Internal").ap()

    S = Sched(nc)
    E = S.emit

    with contextlib.ExitStack() as st:
        def sb(name, shape, dt, stk=st):
            return stk.enter_context(nc.sbuf_tensor(name, shape, dt))

        cst = sb("cst", [128, 4], F32)
        ones_c = sb("ones_c", [128, 1], BF16)
        ones_r = sb("ones_r", [1, 128], BF16)
        srow = sb("srow", [1, 3 * D], F32)
        rowb = sb("rowb", [1, 2 * D], BF16)
        aB = sb("aB", [128, D], BF16)
        bB = sb("bB", [128, D], BF16)

        E("vector", lambda v: v.memset(cst[:, 0:1], EXPB))
        E("vector", lambda v: v.memset(cst[:, 1:2], EPS))
        E("vector", lambda v: v.memset(ones_c[:, :], 1.0))
        cst_v = E("vector", lambda v: v.memset(ones_r[:, :], 1.0), inc=("V", 1))

        def stage(dst, src, rows, semname, nchunks=4):
            step = rows // nchunks
            for i in range(nchunks):
                a, b = i * step, (i + 1) * step
                E("gpsimd", lambda g, a=a, b=b, dst=dst, src=src:
                  g.dma_start(dst[a:b, :], src[a:b, :]), inc=(semname, DM))
            return S.val(semname)

        x_stq = []
        for tbq in range(4):
            a, b = tbq * 2048, (tbq + 1) * 2048
            E("gpsimd", lambda g, a=a, b=b: g.dma_start(x_bf[a:b, :], x_in[a:b, :]),
              inc=(f"x_st{tbq}", DM))
            x_stq.append(S.val(f"x_st{tbq}"))
            if tbq == 0:
                wk_st = stage(wk_bf, wk, D, "wk_st", 2)
                wv_st = stage(wv_bf, wv, D, "wv_st", 2)
        wq_st = stage(wq_bf, wq, D, "wq_st", 2)
        wo_st = stage(wo_bf, wo, D, "wo_st", 2)
        wd_bf = nc.dram_tensor("wd_bf", [FF, D], BF16, kind="Internal").ap()
        wdst_v = None

        def ln_stats_math(n, inv_n):
            s_sum = srow[0:1, 0:n]
            s_sq = srow[0:1, D:D + n]
            t2 = srow[0:1, 2 * D:2 * D + n]
            E("vector", lambda v: v.tensor_scalar_mul(t2, s_sum, inv_n))
            E("vector", lambda v: v.tensor_mul(t2, t2, t2))
            vv = E("vector", lambda v: v.scalar_tensor_tensor(
                t2, s_sq, inv_n, t2, ALU.mult, ALU.subtract), inc=("V", 1))
            cc = E("scalar", lambda se: se.activation(s_sq, t2, AF.Ln, bias=cst[0:1, 1:2]),
                   waits=[("V", vv)], inc=("C", 1))
            cc = E("scalar", lambda se: se.activation(t2, s_sq, AF.Exp, scale=-0.5),
                   inc=("C", 1))
            E("vector", lambda v: v.scalar_tensor_tensor(
                s_sq, s_sum, -inv_n, t2, ALU.mult, ALU.mult), waits=[("C", cc)])
            E("vector", lambda v: v.tensor_copy(rowb[0:1, 0:n], t2))
            vv = E("vector", lambda v: v.tensor_copy(rowb[0:1, D:D + n], s_sq),
                   inc=("V", 1))
            return vv

        def ln_broadcast(n, ps_bc, v_rows, prev_apply_v):
            cc = None
            for c0 in range(0, n, 512):
                for which, dstT in ((0, aB), (1, bB)):
                    row = rowb[0:1, which * D + c0:which * D + c0 + 512]
                    pp = E("tensor", lambda t, row=row: t.matmul(
                        ps_bc[:, :], ones_r[:, :], row, start=True, stop=True),
                        waits=[("V", v_rows), ("C", S.val("C"))], inc=("P", 1))
                    cc = E("scalar", lambda se, dstT=dstT, c0=c0: se.copy(
                        dstT[:, c0:c0 + 512], ps_bc[:, :]),
                        waits=[("P", pp)] + ([("V", prev_apply_v)] if c0 == 0 and which == 0 else []),
                        inc=("C", 1))
            return cc

        v_apply_last = 0

        with contextlib.ExitStack() as stA:
            bigA = sb("bigA", [128, NDT * 2048], BF16, stA)

            with contextlib.ExitStack() as st0:
                sqT = [sb(f"sqT{i}", [128, 2048], BF16, st0) for i in range(2)]
                wTh = [sb(f"wTh{i}", [128, NDT * 1024], BF16, st0) for i in range(2)]
                ksb = [sb(f"ksb{i}", [128, 2048], BF16, st0) for i in range(2)]

                p_end_tb = 0
                v_apply_done = 0
                evac_c = [0, 0]
                store_g = [0, 0]
                p_sqmm = {}

                p_buf_free = [0, 0]

                def load_wTh(wbf, half, buf, stv, stn):
                    for dc in range(NDT):
                        E("sync", lambda s, dc=dc, wbf=wbf, half=half, buf=buf: s.dma_start_transpose(
                            wTh[buf][:, dc * 1024:(dc + 1) * 1024],
                            wbf[half * 1024:(half + 1) * 1024, dc * 128:(dc + 1) * 128]),
                          waits=([(stn, stv), ("P", p_buf_free[buf])]
                                 if dc == 0 else []),
                          inc=(f"wth{buf}", DM))
                    return S.val(f"wth{buf}")

                for tb in range(4):
                    xsem = f"xt{tb % 2}"
                    for dc in range(NDT):
                        E("sync", lambda s, tb=tb, dc=dc: s.dma_start_transpose(
                            bigA[:, dc * 2048:(dc + 1) * 2048],
                            x_bf[tb * 2048:(tb + 1) * 2048, dc * 128:(dc + 1) * 128]),
                          waits=([(f"x_st{tb}", x_stq[tb]), ("P", p_end_tb), ("V", v_apply_done),
                                  ("C", S.val("C")), ("st_o", S.val("st_o"))]
                                 if dc == 0 else []),
                          inc=(xsem, DM))
                    xt_v = S.val(xsem)

                    with (nc.psum_tensor(f"psSum{tb}", [1, 2048], F32) as psSum,
                          nc.psum_tensor(f"psSq{tb}", [1, 2048], F32) as psSq):
                        pp = None
                        for dc in range(NDT):
                            w = [(xsem, xt_v), ("V", cst_v)] if dc == 0 else []
                            if dc >= 2:
                                w = [("P", p_sqmm.get((tb, dc - 2), 0))]
                            vv = E("vector", lambda v, dc=dc: v.tensor_mul(
                                sqT[dc % 2][:, :], bigA[:, dc * 2048:(dc + 1) * 2048],
                                bigA[:, dc * 2048:(dc + 1) * 2048]),
                                waits=w, inc=("V", 1))
                            for c in range(4):
                                E("tensor", lambda t, dc=dc, c=c: t.matmul(
                                    psSum[0:1, c * 512:(c + 1) * 512], ones_c[:, :],
                                    bigA[:, dc * 2048 + c * 512:dc * 2048 + (c + 1) * 512],
                                    start=(dc == 0), stop=(dc == NDT - 1)),
                                    waits=([(xsem, xt_v), ("V", cst_v)] if (dc == 0 and c == 0) else []))
                                pp = E("tensor", lambda t, dc=dc, c=c: t.matmul(
                                    psSq[0:1, c * 512:(c + 1) * 512], ones_c[:, :],
                                    sqT[dc % 2][:, c * 512:(c + 1) * 512],
                                    start=(dc == 0), stop=(dc == NDT - 1)),
                                    waits=([("V", vv)] if c == 0 else []), inc=("P", 1))
                            p_sqmm[(tb, dc)] = pp
                        E("vector", lambda v: v.tensor_copy(srow[0:1, 0:2048], psSum[0:1, :]),
                          waits=[("P", pp)])
                        E("vector", lambda v: v.tensor_copy(srow[0:1, D:2 * D], psSq[0:1, :]),
                          inc=("V", 1))
                    v_rows = ln_stats_math(2048, 1.0 / D)
                    with nc.psum_tensor(f"psBc{tb}", [128, 512], F32) as psBc:
                        cc = ln_broadcast(2048, psBc, v_rows, v_apply_last)
                    for dc in range(NDT):
                        for c in range(4):
                            sl = slice(dc * 2048 + c * 512, dc * 2048 + (c + 1) * 512)
                            E("vector", lambda v, sl=sl, c=c: v.tensor_mul(
                                sqT[0][:, c * 512:(c + 1) * 512], bigA[:, sl],
                                aB[:, c * 512:(c + 1) * 512]),
                                waits=([("C", cc), ("P", S.val("P"))] if (dc == 0 and c == 0) else []))
                            v_apply_done = E("vector", lambda v, sl=sl, c=c: v.tensor_add(
                                bigA[:, sl], sqT[0][:, c * 512:(c + 1) * 512],
                                bB[:, c * 512:(c + 1) * 512]), inc=("V", 1))
                    v_apply_last = v_apply_done

                    def proj_eT(dst_dram, dst_cols, tag, wv_half, buf_of_half):
                        nonlocal evac_c, store_g
                        with (nc.psum_tensor(f"psP{tag}a", [128, 2048], F32) as psPa,
                              nc.psum_tensor(f"psP{tag}b", [128, 2048], F32) as psPb):
                            psP = [psPa, psPb]
                            p_half_done = [0, 0]
                            for eb in range(NDT):
                                half = eb // 8
                                buf = buf_of_half[half]
                                par = eb % 2
                                pp = None
                                for dc in range(NDT):
                                    for c in range(4):
                                        w = []
                                        if dc == 0 and c == 0:
                                            w = [(f"wth{buf}", wv_half[half]),
                                                 ("C", evac_c[par] if eb >= 2 else max(evac_c)),
                                                 ("V", v_apply_done)]
                                        pp = E("tensor", lambda t, dc=dc, c=c, eb=eb, par=par, buf=buf: t.matmul(
                                            psP[par][:, c * 512:(c + 1) * 512],
                                            wTh[buf][:, dc * 1024 + (eb % 8) * 128:dc * 1024 + (eb % 8) * 128 + 128],
                                            bigA[:, dc * 2048 + c * 512:dc * 2048 + (c + 1) * 512],
                                            start=(dc == 0), stop=(dc == NDT - 1)),
                                            waits=w,
                                            inc=(("P", 1) if (dc == NDT - 1 and c == 3) else None))
                                p_half_done[half] = pp
                                cc_l = None
                                for c in range(4):
                                    cc_l = E("scalar", lambda se, c=c, par=par: se.copy(
                                        ksb[par][:, c * 512:(c + 1) * 512],
                                        psP[par][:, c * 512:(c + 1) * 512]),
                                        waits=([("P", pp), ("st_sp", store_g[par])] if c == 0 else []),
                                        inc=("C", 1))
                                evac_c[par] = cc_l
                                store_g[par] = E("gpsimd", lambda g, eb=eb, par=par,
                                                 dst_dram=dst_dram, dst_cols=dst_cols: g.dma_start(
                                    dst_dram[eb * 128:(eb + 1) * 128, dst_cols], ksb[par][:, :]),
                                    waits=[("C", evac_c[par])], inc=("st_sp", DM))
                            return p_half_done

                    def proj_V(tb, wv_half, buf_of_half):
                        nonlocal evac_c, store_g
                        with (nc.psum_tensor(f"psV{tb}a", [128, 1024], F32) as psVa,
                              nc.psum_tensor(f"psV{tb}b", [128, 1024], F32) as psVb):
                            psV = [psVa, psVb]
                            p_half_done = [0, 0]
                            for cp in range(2):
                                buf = buf_of_half[cp]
                                for kb_i in range(16):
                                    par = kb_i % 2
                                    kt_glob = tb * 16 + kb_i
                                    pp = None
                                    for dc in range(NDT):
                                        for c in range(2):
                                            w = []
                                            if dc == 0 and c == 0:
                                                w = [(f"wth{buf}", wv_half[cp]),
                                                     ("C", evac_c[par] if (cp, kb_i) != (0, 0) else max(evac_c))]
                                            pp = E("tensor", lambda t, dc=dc, c=c, kb_i=kb_i, par=par, buf=buf: t.matmul(
                                                psV[par][:, c * 512:(c + 1) * 512],
                                                bigA[:, dc * 2048 + kb_i * 128:dc * 2048 + kb_i * 128 + 128],
                                                wTh[buf][:, dc * 1024 + c * 512:dc * 1024 + (c + 1) * 512],
                                                start=(dc == 0), stop=(dc == NDT - 1)),
                                                waits=w,
                                                inc=(("P", 1) if (dc == NDT - 1 and c == 1) else None))
                                    cc_l = None
                                    for c in range(2):
                                        cc_l = E("scalar", lambda se, c=c, par=par: se.copy(
                                            ksb[par][:, c * 512:(c + 1) * 512],
                                            psV[par][:, c * 512:(c + 1) * 512]),
                                            waits=([("P", pp), ("st_sp", store_g[par])] if c == 0 else []),
                                            inc=("C", 1))
                                    evac_c[par] = cc_l
                                    vdst = Vb[:, :].rearrange("p (h kte) -> p h kte", h=NH)[
                                        :, cp * 8:(cp + 1) * 8, kt_glob * 128:(kt_glob + 1) * 128]
                                    vsrc = ksb[par][:, 0:1024].rearrange("p (h e) -> p h e", e=128)
                                    store_g[par] = E("gpsimd", lambda g, vdst=vdst, vsrc=vsrc: g.dma_start(
                                        vdst, vsrc), waits=[("C", evac_c[par])], inc=("st_sp", DM))
                                p_half_done[cp] = S.val("P")
                            return p_half_done

                    wkh = [0, 0]
                    wkh[0] = load_wTh(wk_bf, 0, 0, wk_st, "wk_st")
                    wkh[1] = load_wTh(wk_bf, 1, 1, wk_st, "wk_st")
                    ph = proj_eT(Kb, slice(tb * 2048, (tb + 1) * 2048), f"k{tb}", wkh, [0, 1])
                    p_buf_free[0] = ph[0]
                    wvh = [0, 0]
                    wvh[0] = load_wTh(wv_bf, 0, 0, wv_st, "wv_st")
                    p_buf_free[1] = ph[1]
                    wvh[1] = load_wTh(wv_bf, 1, 1, wv_st, "wv_st")
                    phv = proj_V(tb, wvh, [0, 1])
                    p_buf_free[0] = phv[0]
                    p_buf_free[1] = phv[1]
                    if tb == 0:
                        wqh = [0, 0]
                        wqh[0] = load_wTh(wq_bf, 0, 0, wq_st, "wq_st")
                        wqh[1] = load_wTh(wq_bf, 1, 1, wq_st, "wq_st")
                        ph = proj_eT(qT_bf, slice(0, 2048), "q", wqh, [0, 1])
                        p_buf_free[0] = ph[0]
                        p_buf_free[1] = ph[1]
                        if "d_xn1T" in dbg_outs:
                            E("gpsimd", lambda g: g.dma_start(dbg_outs["d_xn1T"], bigA[:, :]),
                              waits=[("V", v_apply_done)], inc=("st_o", DM))
                    p_end_tb = S.val("P")

            sp_total = S.val("st_sp")
            wdst_v = stage(wd_bf, wd, FF, "wdst", 8)
            for dn, src_ap in (("d_qT", qT_bf), ("d_kb", Kb), ("d_vb", Vb)):
                if dn in dbg_outs:
                    E("gpsimd", lambda g, dn=dn, src_ap=src_ap: g.dma_start(dbg_outs[dn], src_ap),
                      waits=[("st_sp", sp_total)], inc=("st_o", DM))

            with contextlib.ExitStack() as stF:
                sK = [sb(f"sK{i}", [128, T], BF16, stF) for i in range(2)]
                sV = [sb(f"sV{i}", [128, T], BF16, stF) for i in range(2)]
                sQ = [sb(f"sQ{i}", [128, TQ], BF16, stF) for i in range(2)]
                sE = [sb(f"sE{i}", [128, 1024], BF16, stF) for i in range(8)]
                acc8 = sb("acc8", [128, 1024], BF16, stF)
                accF = sb("accF", [128, 1024], F32, stF)
                accB = sb("accB", [128, 1024], BF16, stF)
                bcF = sb("bcF", [128, 1024], F32, stF)
                recB = sb("recB", [1, 1024], BF16, stF)
                with (nc.psum_tensor("psS0", [128, 1024], F32) as psS0,
                      nc.psum_tensor("psS1", [128, 1024], F32) as psS1,
                      nc.psum_tensor("psO", [128, 1024], F32) as psO,
                      nc.psum_tensor("psX", [128, 1024], F32) as psX):
                    psS = [psS0, psS1]

                    p_head_end = [0, 0]
                    kv_v = [0, 0]

                    def load_head(h):
                        slot = h % 2
                        semn = f"kv{slot}"
                        E("sync", lambda s, h=h, slot=slot: s.dma_start(
                            sK[slot][:, :], Kb[h * 128:(h + 1) * 128, :]),
                          waits=[("st_sp", sp_total), ("P", p_head_end[slot])],
                          inc=(semn, DM))
                        E("sync", lambda s, h=h, slot=slot: s.dma_start(
                            sV[slot][:, :], Vb[:, h * T:(h + 1) * T]), inc=(semn, DM))
                        E("sync", lambda s, h=h, slot=slot: s.dma_start(
                            sQ[slot][:, :], qT_bf[h * 128:(h + 1) * 128, :]), inc=(semn, DM))
                        kv_v[slot] = S.val(semn)

                    load_head(0)
                    load_head(1)

                    pend = None
                    prev_tail = None
                    p_bcast_old = 0
                    p_denmm_old = 0
                    v_attn_hist = 0

                    def ep_den_mm(ep):
                        pp = None
                        for c in range(2):
                            pp = E("tensor", lambda t, c=c: t.matmul(
                                psX[0:1, c * 512:(c + 1) * 512], ones_c[:, :],
                                accB[:, c * 512:(c + 1) * 512], start=True, stop=True),
                                waits=([("V", ep["v_accB"])] if c == 0 else []),
                                inc=("P", 1))
                        return pp

                    def ep_recip(ep):
                        E("vector", lambda v: v.reciprocal(srow[0:1, 0:1024], psX[0:1, 0:1024]),
                          waits=[("P", ep["p_denmm"])])
                        return E("vector", lambda v: v.tensor_copy(recB[0:1, :], srow[0:1, 0:1024]),
                                 waits=[("P", p_bcast_old)], inc=("V", 1))

                    def ep_bcast(ep):
                        pp = None
                        for c in range(2):
                            pp = E("tensor", lambda t, c=c: t.matmul(
                                psX[:, c * 512:(c + 1) * 512], ones_r[:, :],
                                recB[0:1, c * 512:(c + 1) * 512], start=True, stop=True),
                                waits=([("V", ep["v_recB"])] if c == 0 else []), inc=("P", 1))
                        return pp

                    def ep_finish(ep):
                        E("vector", lambda v: v.tensor_copy(bcF[:, :], psX[:, :]),
                          waits=[("P", ep["p_bcast"])])
                        return E("vector", lambda v, h=ep["h"], q0=ep["q0"]: v.tensor_mul(
                            bigA[:, h * 2048 + q0:h * 2048 + q0 + 1024], psO[:, :], bcF[:, :]),
                            waits=[("P", ep["p_pv63"])], inc=("V", 1))

                    for h in range(NH):
                        slot = h % 2
                        semn = f"kv{slot}"
                        kvw = kv_v[slot]
                        for qh in range(2):
                            q0 = qh * 1024
                            c_exp = {}
                            p_s = {}
                            p_pv = {}
                            v_den = {}

                            def emit_S(kt):
                                par = kt % 2
                                if kt == 0:
                                    w = [(semn, kvw)]
                                    if pend is not None:
                                        w.append(("C", pend["c_exp62"] if kt == 0 else 0))
                                elif kt == 1:
                                    w = [("C", pend["c_exp63"])] if pend is not None else []
                                else:
                                    w = [("C", c_exp[kt - 2])]
                                E("tensor", lambda t, kt=kt, par=par, slot=slot, q0=q0: t.matmul(
                                    psS[par][:, 0:512], sK[slot][:, kt * 128:(kt + 1) * 128],
                                    sQ[slot][:, q0:q0 + 512], start=True, stop=True), waits=w)
                                p_s[kt] = E("tensor", lambda t, kt=kt, par=par, slot=slot, q0=q0: t.matmul(
                                    psS[par][:, 512:1024], sK[slot][:, kt * 128:(kt + 1) * 128],
                                    sQ[slot][:, q0 + 512:q0 + 1024], start=True, stop=True),
                                    inc=("P", 1))

                            def emit_exp(kt):
                                pe8 = kt % 8
                                w = [("P", p_s[kt])]
                                if kt >= 8:
                                    w += [("P", p_pv[kt - 8]), ("V", v_den[kt - 8])]
                                elif prev_tail is not None:
                                    w += [("P", prev_tail[0][56 + kt]), ("V", prev_tail[1][56 + kt])]
                                c_exp[kt] = E("scalar", lambda se, kt=kt, pe8=pe8: se.activation(
                                    sE[pe8][:, :], psS[kt % 2][:, :], AF.Exp,
                                    bias=cst[:, 0:1], scale=SCALE), waits=w, inc=("C", 1))

                            def emit_PV(kt):
                                pe8 = kt % 8
                                w = [("C", c_exp[kt])]
                                if kt == 0:
                                    w.append(("V", v_attn_hist))
                                E("tensor", lambda t, kt=kt, pe8=pe8, slot=slot: t.matmul(
                                    psO[:, 0:512], sV[slot][:, kt * 128:(kt + 1) * 128],
                                    sE[pe8][:, 0:512], start=(kt == 0), stop=(kt == 63)),
                                    waits=w)
                                p_pv[kt] = E("tensor", lambda t, kt=kt, pe8=pe8, slot=slot: t.matmul(
                                    psO[:, 512:1024], sV[slot][:, kt * 128:(kt + 1) * 128],
                                    sE[pe8][:, 512:1024], start=(kt == 0), stop=(kt == 63)),
                                    inc=("P", 1))

                            def emit_den(kt):
                                pe8 = kt % 8
                                w = [("C", c_exp[kt])]
                                ops = []
                                if kt % 8 == 0:
                                    ops.append(lambda v, pe8=pe8: v.tensor_copy(acc8[:, :], sE[pe8][:, :]))
                                else:
                                    ops.append(lambda v, pe8=pe8: v.tensor_add(
                                        acc8[:, :], acc8[:, :], sE[pe8][:, :]))
                                if kt % 8 == 7:
                                    if kt == 7:
                                        ops.append(lambda v: v.tensor_copy(accF[:, :], acc8[:, :]))
                                    else:
                                        ops.append(lambda v: v.tensor_add(accF[:, :], accF[:, :], acc8[:, :]))
                                for i, op in enumerate(ops):
                                    last = i == len(ops) - 1
                                    v_den[kt] = E("vector", op, waits=(w if i == 0 else []),
                                                  inc=(("V", 1) if last else None))

                            for i in range(70):
                                if 0 <= i - 2 <= 63:
                                    emit_exp(i - 2)
                                if i <= 63:
                                    emit_S(i)
                                if pend is not None:
                                    if i == 2:
                                        p_denmm_old = ep_den_mm(pend)
                                        pend["p_denmm"] = p_denmm_old
                                    elif i == 3:
                                        pend["v_recB"] = ep_recip(pend)
                                    elif i == 4:
                                        p_bcast_new = ep_bcast(pend)
                                        pend["p_bcast"] = p_bcast_new
                                    elif i == 5:
                                        v_attn_hist = ep_finish(pend)
                                        p_bcast_old = pend["p_bcast"]
                                        pend = None
                                if 0 <= i - 6 <= 63:
                                    emit_PV(i - 6)
                                if 0 <= i - 2 <= 63:
                                    emit_den(i - 2)

                            v_accB = E("vector", lambda v: v.tensor_copy(accB[:, :], accF[:, :]),
                                       waits=[("V", v_den[63]), ("P", p_denmm_old)], inc=("V", 1))
                            pend = dict(h=h, q0=q0, v_accB=v_accB,
                                        c_exp62=c_exp[62], c_exp63=c_exp[63],
                                        p_pv63=p_pv[63])
                            prev_tail = (p_pv, v_den)
                        p_head_end[slot] = S.val("P")
                        if h + 2 < NH:
                            load_head(h + 2)

                    p_denmm_old = ep_den_mm(pend)
                    pend["p_denmm"] = p_denmm_old
                    pend["v_recB"] = ep_recip(pend)
                    pend["p_bcast"] = ep_bcast(pend)
                    v_attn_hist = ep_finish(pend)
                    pend = None

            flash_v = S.val("V")
            if "d_attnT" in dbg_outs:
                E("gpsimd", lambda g: g.dma_start(dbg_outs["d_attnT"], bigA[:, :]),
                  waits=[("V", flash_v)], inc=("st_o", DM))

            with contextlib.ExitStack() as stO:
                wTo = sb("wTo", [128, NDT * 2048], BF16, stO)
                tmp_o = sb("tmp_o", [128, 16 * 512], BF16, stO)
                sq5 = sb("sq5", [128, 512], BF16, stO)
                xrl = [sb(f"xrl{i}", [128, 512], BF16, stO) for i in range(2)]
                t5 = sb("t5o", [128, 512], BF16, stO)

                bP, bV = S.val("P"), S.val("V")
                for dc in range(NDT):
                    E("sync", lambda s, dc=dc: s.dma_start_transpose(
                        wTo[:, dc * 2048:(dc + 1) * 2048], wo_bf[:, dc * 128:(dc + 1) * 128]),
                      waits=([("wo_st", wo_st), ("P", bP), ("V", bV)] if dc == 0 else []),
                      inc=("wTo_s", DM))
                wto_v = S.val("wTo_s")

                v_wb_prev = 0
                p_sq_prev = 0
                with (nc.psum_tensor("psPo0", [128, 512], F32) as psPo0,
                      nc.psum_tensor("psPo1", [128, 512], F32) as psPo1,
                      nc.psum_tensor("psL2s", [1, 512], F32) as psL2s,
                      nc.psum_tensor("psL2q", [1, 512], F32) as psL2q):
                    psPo = [psPo0, psPo1]
                    evc = [0, 0]
                    v_stat_ev = 0
                    for tc in range(4):
                        for eg in range(NDT):
                            par = eg % 2
                            pp = None
                            for dc in range(NDT):
                                pp = E("tensor", lambda t, dc=dc, eg=eg, tc=tc, par=par: t.matmul(
                                    psPo[par][:, :],
                                    wTo[:, dc * 2048 + eg * 128:dc * 2048 + eg * 128 + 128],
                                    bigA[:, dc * 2048 + tc * 512:dc * 2048 + (tc + 1) * 512],
                                    start=(dc == 0), stop=(dc == NDT - 1)),
                                    waits=([("wTo_s", wto_v), ("C", evc[par]), ("V", flash_v)]
                                           if dc == 0 else []),
                                    inc=(("P", 1) if dc == NDT - 1 else None))
                            cc1 = E("scalar", lambda se, eg=eg, par=par: se.copy(
                                tmp_o[:, eg * 512:(eg + 1) * 512], psPo[par][:, :]),
                                waits=[("P", pp)] + ([("V", v_wb_prev)] if eg == 0 else []),
                                inc=("C", 1))
                            evc[par] = cc1
                            vv = E("vector", lambda v, eg=eg: v.tensor_mul(
                                sq5[:, :], tmp_o[:, eg * 512:(eg + 1) * 512],
                                tmp_o[:, eg * 512:(eg + 1) * 512]),
                                waits=[("C", cc1), ("P", p_sq_prev)], inc=("V", 1))
                            E("tensor", lambda t, eg=eg: t.matmul(
                                psL2s[0:1, 0:512], ones_c[:, :],
                                tmp_o[:, eg * 512:(eg + 1) * 512],
                                start=(eg == 0), stop=(eg == 15)),
                                waits=([("C", cc1)] + ([("V", v_stat_ev)] if eg == 0 else [])))
                            p_sq_prev = E("tensor", lambda t, eg=eg: t.matmul(
                                psL2q[0:1, 0:512], ones_c[:, :], sq5[:, :],
                                start=(eg == 0), stop=(eg == 15)),
                                waits=[("V", vv)], inc=("P", 1))
                        E("vector", lambda v, tc=tc: v.tensor_copy(
                            srow[0:1, tc * 512:(tc + 1) * 512], psL2s[0:1, :]),
                            waits=[("P", p_sq_prev)])
                        v_stat_ev = E("vector", lambda v, tc=tc: v.tensor_copy(
                            srow[0:1, D + tc * 512:D + (tc + 1) * 512], psL2q[0:1, :]),
                            inc=("V", 1))
                        for dt_ in range(NDT):
                            v_wb_prev = E("vector", lambda v, dt_=dt_, tc=tc: v.tensor_copy(
                                bigA[:, dt_ * 2048 + tc * 512:dt_ * 2048 + (tc + 1) * 512],
                                tmp_o[:, dt_ * 512:(dt_ + 1) * 512]),
                                waits=([("P", S.val("P")), ("st_o", S.val("st_o"))] if dt_ == 0 else []),
                                inc=("V", 1))
                    v_oA_wb = v_wb_prev
                    v_rows2 = ln_stats_math(2048, 1.0 / D)
                if "d_oA" in dbg_outs:
                    E("gpsimd", lambda g: g.dma_start(dbg_outs["d_oA"], bigA[:, :]),
                      waits=[("V", v_oA_wb)], inc=("st_o", DM))

                with nc.psum_tensor("psBcO2", [128, 512], F32) as psBc2:
                    cc2 = ln_broadcast(2048, psBc2, v_rows2, v_apply_last)

                v_xrl_used = [0, 0]
                x1st_v = 0
                with (nc.psum_tensor("psL3s", [1, 512], F32) as psL3s,
                      nc.psum_tensor("psL3q", [1, 512], F32) as psL3q):
                    v_stat_ev3 = 0
                    for tc in range(4):
                        for dt_ in range(NDT):
                            slot = (tc * NDT + dt_) % 2
                            xv = E("sync", lambda s, dt_=dt_, tc=tc, slot=slot: s.dma_start_transpose(
                                xrl[slot][:, :],
                                x_bf[tc * 512:(tc + 1) * 512, dt_ * 128:(dt_ + 1) * 128]),
                                waits=[("V", v_xrl_used[slot])], inc=(f"xrl{slot}", DM))
                            sl = slice(dt_ * 2048 + tc * 512, dt_ * 2048 + (tc + 1) * 512)
                            E("vector", lambda v, sl=sl, tc=tc: v.tensor_mul(
                                t5[:, :], bigA[:, sl], aB[:, tc * 512:(tc + 1) * 512]),
                                waits=([("C", cc2), ("st_o", S.val("st_o"))] if (tc == 0 and dt_ == 0) else []))
                            E("vector", lambda v, tc=tc: v.tensor_add(
                                t5[:, :], t5[:, :], bB[:, tc * 512:(tc + 1) * 512]))
                            vv = E("vector", lambda v, sl=sl, slot=slot: v.tensor_add(
                                bigA[:, sl], t5[:, :], xrl[slot][:, :]),
                                waits=[(f"xrl{slot}", xv)], inc=("V", 1))
                            v_xrl_used[slot] = vv
                            x1st_v = E("gpsimd", lambda g, dt_=dt_, tc=tc, sl=sl: g.dma_start(
                                x1_bf[dt_ * 128:(dt_ + 1) * 128, tc * 512:(tc + 1) * 512],
                                bigA[:, sl]), waits=[("V", vv)], inc=("x1_st", DM))
                            vv2 = E("vector", lambda v, sl=sl: v.tensor_mul(
                                sq5[:, :], bigA[:, sl], bigA[:, sl]),
                                waits=[("P", p_sq_prev)], inc=("V", 1))
                            E("tensor", lambda t, sl=sl, dt_=dt_: t.matmul(
                                psL3s[0:1, 0:512], ones_c[:, :], bigA[:, sl],
                                start=(dt_ == 0), stop=(dt_ == NDT - 1)),
                                waits=([("V", vv)] + ([("V", v_stat_ev3)] if dt_ == 0 else [])))
                            p_sq_prev = E("tensor", lambda t, dt_=dt_: t.matmul(
                                psL3q[0:1, 0:512], ones_c[:, :], sq5[:, :],
                                start=(dt_ == 0), stop=(dt_ == NDT - 1)),
                                waits=[("V", vv2)], inc=("P", 1))
                        E("vector", lambda v, tc=tc: v.tensor_copy(
                            srow[0:1, tc * 512:(tc + 1) * 512], psL3s[0:1, :]),
                            waits=[("P", p_sq_prev)])
                        v_stat_ev3 = E("vector", lambda v, tc=tc: v.tensor_copy(
                            srow[0:1, D + tc * 512:D + (tc + 1) * 512], psL3q[0:1, :]),
                            inc=("V", 1))
                    v_rows3 = ln_stats_math(2048, 1.0 / D)
                if "d_x1" in dbg_outs:
                    E("gpsimd", lambda g: g.dma_start(dbg_outs["d_x1"], x1_bf),
                      waits=[("x1_st", x1st_v)], inc=("st_o", DM))
                with nc.psum_tensor("psBcO3", [128, 512], F32) as psBc3:
                    cc3 = ln_broadcast(2048, psBc3, v_rows3, v_apply_last)
                v_xn2 = 0
                for dt_ in range(NDT):
                    for c in range(4):
                        sl = slice(dt_ * 2048 + c * 512, dt_ * 2048 + (c + 1) * 512)
                        E("vector", lambda v, sl=sl, c=c: v.tensor_mul(
                            t5[:, :], bigA[:, sl], aB[:, c * 512:(c + 1) * 512]),
                            waits=([("C", cc3), ("x1_st", x1st_v)] if (dt_ == 0 and c == 0) else []))
                        v_xn2 = E("vector", lambda v, sl=sl, c=c: v.tensor_add(
                            bigA[:, sl], t5[:, :], bB[:, c * 512:(c + 1) * 512]), inc=("V", 1))
                v_apply_last = v_xn2
            if "d_xn2T" in dbg_outs:
                E("gpsimd", lambda g: g.dma_start(dbg_outs["d_xn2T"], bigA[:, :]),
                  waits=[("V", v_xn2)], inc=("st_o", DM))

            with contextlib.ExitStack() as st1:
                wgs = [sb(f"wgs{i}", [128, NDT * 512], BF16, st1) for i in range(2)]
                wus = [sb(f"wus{i}", [128, NDT * 512], BF16, st1) for i in range(2)]
                hq = sb("hq", [128, 4 * 512], F32, st1)
                ha = sb("ha", [128, 4 * 512], F32, st1)
                hm = sb("hm", [128, 3 * 512], F32, st1)
                hsp4 = sb("hsp4", [128, 4 * 512], BF16, st1)
                sg = sb("sg", [128, 512], F32, st1)

                bP1, bV1, bC1 = S.val("P"), S.val("V"), S.val("C")
                p_ib_end = [bP1, bP1]

                def load_wgu(ib):
                    slot = ib % 2
                    semn = f"wgu{slot}"
                    src_g = wg[:, ib * 512:(ib + 1) * 512].rearrange("(hc p) i -> p hc i", p=128)
                    dst_g = wgs[slot][:, :].rearrange("p (hc i) -> p hc i", i=512)
                    E("gpsimd", lambda g, src_g=src_g, dst_g=dst_g: g.dma_start(dst_g, src_g),
                      waits=[("P", p_ib_end[slot])] + ([("V", bV1), ("C", bC1)] if ib < 2 else []),
                      inc=(semn, DM))
                    src_u = wu[:, ib * 512:(ib + 1) * 512].rearrange("(hc p) i -> p hc i", p=128)
                    dst_u = wus[slot][:, :].rearrange("p (hc i) -> p hc i", i=512)
                    E("gpsimd", lambda g, src_u=src_u, dst_u=dst_u: g.dma_start(dst_u, src_u),
                      inc=(semn, DM))
                    return S.val(semn)

                wguv = [load_wgu(0), load_wgu(1)]
                with (nc.psum_tensor("psG", [128, 2048], F32) as psG,
                      nc.psum_tensor("psU", [128, 2048], F32) as psU):
                    c_half = [0, 0]
                    v_half = [0, 0]
                    v_spars_last = 0
                    hsp_store_v = 0
                    for ib in range(16):
                        slot = ib % 2
                        for tc in range(4):
                            for jh in range(2):
                                pp = None
                                for j in (jh * 2, jh * 2 + 1):
                                    for dcw, ps in ((wgs[slot], psG), (wus[slot], psU)):
                                        for hc in range(NDT):
                                            lhsT = dcw[:, hc * 512 + j:hc * 512 + j + 509:4]
                                            w = []
                                            if hc == 0:
                                                w = [("C", c_half[jh]), ("V", v_half[jh])]
                                                if j == jh * 2 and ps is psG:
                                                    w.append((f"wgu{slot}", wguv[slot]))
                                                    if ib == 0 and tc == 0:
                                                        w.append(("V", v_xn2))
                                            pp = E("tensor", lambda t, j=j, tc=tc, ps=ps, lhsT=lhsT, hc=hc: t.matmul(
                                                ps[:, j * 512:(j + 1) * 512], lhsT,
                                                bigA[:, hc * 2048 + tc * 512:hc * 2048 + (tc + 1) * 512],
                                                start=(hc == 0), stop=(hc == NDT - 1)), waits=w,
                                                inc=(("P", 1) if hc == NDT - 1 else None))
                                vj = v_half[jh]
                                for j in (jh * 2, jh * 2 + 1):
                                    sg_guard = v_spars_last if jh == 0 else v_half[0]
                                    cc = E("scalar", lambda se, j=j: se.activation(
                                        sg[:, :], psG[:, j * 512:(j + 1) * 512], AF.Silu),
                                        waits=([("P", pp), ("V", sg_guard)] if j == jh * 2
                                               else [("V", vj)]),
                                        inc=("C", 1))
                                    vj = E("vector", lambda v, j=j: v.tensor_mul(
                                        hq[:, j * 512:(j + 1) * 512], sg[:, :],
                                        psU[:, j * 512:(j + 1) * 512]),
                                        waits=[("C", cc)], inc=("V", 1))
                                    cc2 = E("scalar", lambda se, j=j: se.activation(
                                        ha[:, j * 512:(j + 1) * 512],
                                        hq[:, j * 512:(j + 1) * 512], AF.Abs),
                                        waits=[("V", vj)], inc=("C", 1))
                                c_half[jh] = cc2
                                v_half[jh] = vj
                            a0, a1 = ha[:, 0:512], ha[:, 512:1024]
                            a2, a3 = ha[:, 1024:1536], ha[:, 1536:2048]
                            h0, h1, h2 = hm[:, 0:512], hm[:, 512:1024], hm[:, 1024:1536]
                            E("vector", lambda v: v.tensor_max(h0, a0, a1),
                              waits=[("C", c_half[1]), ("hsp_st", hsp_store_v)])
                            E("vector", lambda v: v.tensor_tensor(h1, a0, a1, ALU.min))
                            E("vector", lambda v: v.tensor_max(h2, a2, a3))
                            E("vector", lambda v: v.tensor_tensor(h0, h0, h2, ALU.min))
                            E("vector", lambda v: v.tensor_tensor(h2, a2, a3, ALU.min))
                            E("vector", lambda v: v.tensor_max(h1, h1, h2))
                            E("vector", lambda v: v.tensor_max(h0, h0, h1))
                            vlast = None
                            for j in range(4):
                                E("vector", lambda v, j=j: v.tensor_tensor(
                                    h1[:, :], ha[:, j * 512:(j + 1) * 512], h0, ALU.is_ge))
                                vlast = E("vector", lambda v, j=j: v.tensor_mul(
                                    hsp4[:, j * 512:(j + 1) * 512],
                                    hq[:, j * 512:(j + 1) * 512], h1[:, :]), inc=("V", 1))
                            v_spars_last = vlast
                            for j in range(4):
                                hsp_store_v = E("gpsimd", lambda g, ib=ib, j=j, tc=tc: g.dma_start(
                                    hsp_bf[(ib * 4 + j) * 128:(ib * 4 + j + 1) * 128,
                                           tc * 512:(tc + 1) * 512],
                                    hsp4[:, j * 512:(j + 1) * 512]),
                                    waits=([("V", vlast)] if j == 0 else []), inc=("hsp_st", DM))
                        p_ib_end[slot] = S.val("P")
                        if ib + 2 < 16:
                            wguv[slot] = load_wgu(ib + 2)
            hsp_total = S.val("hsp_st")
            if "d_hsp" in dbg_outs:
                E("gpsimd", lambda g: g.dma_start(dbg_outs["d_hsp"], hsp_bf),
                  waits=[("hsp_st", hsp_total)], inc=("st_o", DM))

        with contextlib.ExitStack() as st2:
            hspS = [sb(f"hspS{i}", [128, 16 * 1024], BF16, st2) for i in range(2)]
            wds = [sb(f"wds{i}", [128, 8 * 512], BF16, st2) for i in range(2)]
            yb = [sb(f"yb{i}", [128, NDT * 1024], BF16, st2) for i in range(2)]
            sqy = sb("sqy", [128, 1024], BF16, st2)
            x1r = [sb(f"x1r{i}", [128, 1024], BF16, st2) for i in range(2)]
            t6 = sb("t6", [128, 1024], BF16, st2)
            oT = [sb(f"oT{i}", [128, 1024], BF16, st2) for i in range(2)]
            finB = sb("finB", [128, 2 * 2048], BF16, st2)

            bP2, bV2, bC2 = S.val("P"), S.val("V"), S.val("C")
            out_total = 0
            v_app4 = 0
            fin_used = [0, 0]
            v_x1r_used = [bV2, bV2]
            vo_used = [0, 0]
            p_hsp_used = [bP2, bP2]
            hsp_vv = [0, 0]
            p_wd = [bP2, bP2]
            wd_v = [0, 0]

            def load_hsp_slab(tc, q):
                slot = q % 2
                semn = f"hspl{slot}"
                hsrc = hsp_bf[q * 16 * 128:(q + 1) * 16 * 128,
                              tc * 1024:(tc + 1) * 1024].rearrange(
                    "(iblk p) t -> p iblk t", p=128)
                hdst = hspS[slot][:, :].rearrange("p (iblk t) -> p iblk t", t=1024)
                E("sync", lambda s, hsrc=hsrc, hdst=hdst: s.dma_start(hdst, hsrc),
                  waits=[("hsp_st", hsp_total), ("P", p_hsp_used[slot])],
                  inc=(semn, DM))
                hsp_vv[slot] = S.val(semn)

            def load_wd(grp, eg):
                slot = grp % 2
                semn = f"wd{slot}"
                for ibl in range(2):
                    srcap = wd_bf[grp * 1024 + ibl * 512:grp * 1024 + (ibl + 1) * 512,
                                  eg * 512:(eg + 1) * 512].rearrange(
                        "(p four) e -> p four e", four=4)
                    dstap = wds[slot][:, ibl * 2048:(ibl + 1) * 2048].rearrange(
                        "p (four e) -> p four e", e=512)
                    E("gpsimd", lambda g, srcap=srcap, dstap=dstap: g.dma_start(dstap, srcap),
                      waits=([("P", p_wd[slot])] + ([("wdst", wdst_v)] if (grp < 2 and eg == 0) else [])
                             if ibl == 0 else []),
                      inc=(semn, DM))
                return S.val(semn)

            for tc in range(2):
                with contextlib.ExitStack() as stY:
                    psY = [stY.enter_context(nc.psum_tensor(f"psY{tc}_{i}", [128, 1024], F32))
                           for i in range(4)]
                    c_evY = S.val("C") if tc > 0 else 0
                    for eg in range(4):
                        load_hsp_slab(tc, 0)
                        load_hsp_slab(tc, 1)
                        wd_v[0] = load_wd(0, eg)
                        wd_v[1] = load_wd(1, eg)
                        pp = None
                        for q in range(4):
                            hslot = q % 2
                            for sub in range(2):
                                grp = q * 2 + sub
                                wslot = grp % 2
                                for ibl2 in range(2):
                                    for j in range(4):
                                        iblk = grp * 8 + ibl2 * 4 + j
                                        ibl_loc = (iblk - q * 16)
                                        for eb in range(4):
                                            for th in range(2):
                                                w = []
                                                if ibl2 == 0 and j == 0 and eb == 0 and th == 0:
                                                    w = [(f"wd{wslot}", wd_v[wslot]),
                                                         (f"hspl{hslot}", hsp_vv[hslot])]
                                                    if grp == 0:
                                                        w.append(("C", c_evY))
                                                        w.append(("V", v_app4))
                                                pp = E("tensor", lambda t, iblk=iblk, ibl2=ibl2, j=j,
                                                       eb=eb, th=th, wslot=wslot, hslot=hslot,
                                                       ibl_loc=ibl_loc: t.matmul(
                                                    psY[eb][:, th * 512:(th + 1) * 512],
                                                    wds[wslot][:, ibl2 * 2048 + j * 512 + eb * 128:
                                                               ibl2 * 2048 + j * 512 + eb * 128 + 128],
                                                    hspS[hslot][:, ibl_loc * 1024 + th * 512:
                                                                ibl_loc * 1024 + (th + 1) * 512],
                                                    start=(iblk == 0), stop=(iblk == 63)),
                                                    waits=w,
                                                    inc=(("P", 1) if (ibl2 == 1 and j == 3 and eb == 3 and th == 1) else None))
                                p_wd[wslot] = S.val("P")
                                if grp + 2 < 8:
                                    wd_v[wslot] = load_wd(grp + 2, eg)
                            p_hsp_used[hslot] = S.val("P")
                            if q + 2 < 4:
                                load_hsp_slab(tc, q + 2)
                        for eb in range(4):
                            c_evY = E("scalar", lambda se, eb=eb, eg=eg: se.copy(
                                yb[tc % 2][:, (eg * 4 + eb) * 1024:(eg * 4 + eb + 1) * 1024],
                                psY[eb][:, :]),
                                waits=([("P", pp)] if eb == 0 else []), inc=("C", 1))

                with (nc.psum_tensor(f"psL4s{tc}", [1, 1024], F32) as psL4s,
                      nc.psum_tensor(f"psL4q{tc}", [1, 1024], F32) as psL4q,
                      nc.psum_tensor(f"psBc4{tc}", [128, 512], F32) as psBc4):
                    p_sqy = 0
                    for et in range(NDT):
                        vv = E("vector", lambda v, et=et: v.tensor_mul(
                            sqy[:, :], yb[tc % 2][:, et * 1024:(et + 1) * 1024],
                            yb[tc % 2][:, et * 1024:(et + 1) * 1024]),
                            waits=[("C", c_evY), ("P", p_sqy)], inc=("V", 1))
                        for c in range(2):
                            E("tensor", lambda t, et=et, c=c: t.matmul(
                                psL4s[0:1, c * 512:(c + 1) * 512], ones_c[:, :],
                                yb[tc % 2][:, et * 1024 + c * 512:et * 1024 + (c + 1) * 512],
                                start=(et == 0), stop=(et == NDT - 1)),
                                waits=([("C", c_evY)] if (et == 0 and c == 0) else []))
                            p_sqy = E("tensor", lambda t, et=et, c=c: t.matmul(
                                psL4q[0:1, c * 512:(c + 1) * 512], ones_c[:, :],
                                sqy[:, c * 512:(c + 1) * 512],
                                start=(et == 0), stop=(et == NDT - 1)),
                                waits=([("V", vv)] if c == 0 else []),
                                inc=(("P", 1) if (et == NDT - 1 and c == 1) else None))
                    E("vector", lambda v: v.tensor_copy(srow[0:1, 0:1024], psL4s[0:1, :]),
                      waits=[("P", S.val("P"))])
                    E("vector", lambda v: v.tensor_copy(srow[0:1, D:D + 1024], psL4q[0:1, :]),
                      inc=("V", 1))
                    v_rows4 = ln_stats_math(1024, 1.0 / D)
                    cc4 = ln_broadcast(1024, psBc4, v_rows4, v_apply_last)
                    for et in range(NDT):
                        slot = et % 2
                        xv = E("sync", lambda s, et=et, tc=tc, slot=slot: s.dma_start(
                            x1r[slot][:, :],
                            x1_bf[et * 128:(et + 1) * 128, tc * 1024:(tc + 1) * 1024]),
                            waits=[("V", v_x1r_used[slot])] + ([("x1_st", S.val("x1_st"))] if et < 2 and tc == 0 else []),
                            inc=(f"x1r{slot}", DM))
                        for c in range(2):
                            E("vector", lambda v, et=et, c=c: v.tensor_mul(
                                t6[:, c * 512:(c + 1) * 512],
                                yb[tc % 2][:, et * 1024 + c * 512:et * 1024 + (c + 1) * 512],
                                aB[:, c * 512:(c + 1) * 512]),
                                waits=([("C", cc4)] if (et == 0 and c == 0) else []))
                            E("vector", lambda v, c=c: v.tensor_add(
                                t6[:, c * 512:(c + 1) * 512], t6[:, c * 512:(c + 1) * 512],
                                bB[:, c * 512:(c + 1) * 512]))
                        vv = E("vector", lambda v, slot=slot: v.tensor_add(
                            oT[slot][:, :], t6[:, :], x1r[slot][:, :]),
                            waits=[(f"x1r{slot}", xv), ("outT_st", vo_used[slot])], inc=("V", 1))
                        v_x1r_used[slot] = vv
                        v_app4 = vv
                        out_total = E("gpsimd", lambda g, et=et, tc=tc, slot=slot: g.dma_start(
                            outT_bf[et * 128:(et + 1) * 128, tc * 1024:(tc + 1) * 1024],
                            oT[slot][:, :]), waits=[("V", vv)], inc=("outT_st", DM))
                        vo_used[slot] = out_total

                for tt in range(tc * 8, (tc + 1) * 8):
                    slot = tt % 2
                    tv = E("sync", lambda s, tt=tt, slot=slot: s.dma_start_transpose(
                        finB[:, slot * 2048:(slot + 1) * 2048],
                        outT_bf[:, tt * 128:(tt + 1) * 128]),
                        waits=[("outT_st", out_total), ("out_st", fin_used[slot])],
                        inc=(f"fin{slot}", DM))
                    fin_used[slot] = E("gpsimd", lambda g, tt=tt, slot=slot: g.dma_start(
                        out[tt * 128:(tt + 1) * 128, :], finB[:, slot * 2048:(slot + 1) * 2048]),
                        waits=[(f"fin{slot}", tv)], inc=("out_st", DM))

            if "d_outT" in dbg_outs:
                E("gpsimd", lambda g: g.dma_start(dbg_outs["d_outT"], outT_bf),
                  waits=[("outT_st", out_total)], inc=("st_o", DM))
            E("sync", None, waits=[("out_st", S.val("out_st")), ("st_o", S.val("st_o"))])
            E("gpsimd", None, waits=[("out_st", S.val("out_st"))])

        S.build_block()
        S.close()
    return nc


_NC_CACHE = {}


def kernel(L0_attn_x, L0_attn_w_q, L0_attn_w_k, L0_attn_w_v, L0_attn_w_o,
           L0_ffn_w_gate, L0_ffn_w_up, L0_ffn_w_down):
    x = np.ascontiguousarray(np.asarray(L0_attn_x, dtype=np.float32))
    ws = {k: np.ascontiguousarray(np.asarray(v, dtype=np.float32)) for k, v in dict(
        wq=L0_attn_w_q, wk=L0_attn_w_k, wv=L0_attn_w_v, wo=L0_attn_w_o,
        wg=L0_ffn_w_gate, wu=L0_ffn_w_up, wd=L0_ffn_w_down).items()}
    if "main" not in _NC_CACHE:
        _NC_CACHE["main"] = build_program()
    nc = _NC_CACHE["main"]
    in_maps = []
    for c in range(8):
        b, qi = c // 4, c % 4
        xb = np.roll(x[b], -qi * TQ, axis=0)
        m = dict(x=np.ascontiguousarray(xb))
        m.update(ws)
        in_maps.append(m)
    res = run_bass_kernel_spmd(nc, in_maps, core_ids=list(range(8)))
    full = np.empty((2, T, D), dtype=np.float32)
    for c in range(8):
        b, qi = c // 4, c % 4
        full[b, qi * TQ:(qi + 1) * TQ, :] = res.results[c]["out"]
    return full
